# version 25
# baseline (speedup 1.0000x reference)
import numpy as np
import ml_dtypes
import concourse.bass as bass
import concourse.mybir as mybir
from concourse.bass_utils import run_bass_kernel_spmd

F32 = mybir.dt.float32
BF16 = mybir.dt.bfloat16
AF = mybir.ActivationFunctionType
ALU = mybir.AluOpType
AX = mybir.AxisListType

NCORES = 8
B_CORE = 2
T = 2048
D = 1024
H = 16
HD = 64
TOPK = 32
NCH = 8
NTT = 16
STRIP_W = 3968


class _It:
    __slots__ = ("eng", "fn", "deps", "dma", "sig", "val", "waits", "idx")

    def __init__(self, eng, fn, deps, dma):
        self.eng = eng
        self.fn = fn
        self.deps = deps
        self.dma = dma
        self.sig = False
        self.val = 0
        self.waits = ()
        self.idx = -1


class Buf:
    __slots__ = ("name", "w", "rs")

    def __init__(self, name):
        self.name = name
        self.w = None
        self.rs = {}


class Plan:
    def __init__(self):
        self.items = []
        self.fence_deps = []
        self.fence_new = set()
        self.store_items = []

    def op(self, eng, fn, reads=(), writes=(), dma=None, extra_deps=(),
           soft_reads=()):
        deps = []
        soft = []
        for b in reads:
            if b.w is not None:
                deps.append(b.w)
        for b in soft_reads:
            if b.w is not None:
                if b.w.eng == eng and b.w.dma is None:
                    soft.append(b.w)
                else:
                    deps.append(b.w)
        for b in writes:
            deps.extend(b.rs.values())
            if b.w is not None:
                deps.append(b.w)
        deps.extend(extra_deps)
        if eng in self.fence_new:
            deps.extend(self.fence_deps)
            self.fence_new.discard(eng)
        it = _It(eng, fn, deps, dma)
        it.idx = len(self.items)
        self.items.append(it)
        for b in reads:
            b.rs[eng] = it
        for b in soft_reads:
            b.rs[eng] = it
        for b in writes:
            b.w = it
            b.rs = {}
        return it

    def fence(self):
        last = {}
        for it in self.items:
            last[it.eng] = it
        self.fence_deps = list(last.values())
        self.fence_new = {"v", "s", "t", "g", "y"}

    def finalize(self):
        for it in self.items:
            if it.dma in ("dSO", "dSO1"):
                it.sig = True
            for d in it.deps:
                if d is it:
                    continue
                if d.eng == "t" and it.eng == "t" and d.dma is None:
                    continue
                d.sig = True
        cnt = {}
        for it in self.items:
            if it.sig:
                key = it.dma if it.dma else ("E", it.eng)
                cnt[key] = cnt.get(key, 0) + 1
                it.val = cnt[key] * (16 if it.dma else 1)
        known = {}
        for it in self.items:
            k = known.setdefault(it.eng, {})
            ws = {}
            for d in it.deps:
                if d is it:
                    continue
                if d.eng == "t" and it.eng == "t" and d.dma is None:
                    continue
                if not d.sig:
                    continue
                key = d.dma if d.dma else ("E", d.eng)
                if ws.get(key, 0) < d.val:
                    ws[key] = d.val
            final = []
            for key, v in ws.items():
                if k.get(key, 0) < v:
                    k[key] = v
                    final.append((key, v))
            it.waits = tuple(final)
        return cnt


def build_program(debug=False):
    nc = bass.Bass("TRN2", target_bir_lowering=False, debug=False)

    dt = {}
    def din(name, shape, dtype=F32):
        dt[name] = nc.dram_tensor(name, shape, dtype, kind="ExternalInput").ap()
        return dt[name]

    x_h = din("x", [B_CORE, T, D])
    wq_h = din("wqT", [D, D]); wk_h = din("wkT", [D, D]); wv_h = din("wvT", [D, D])
    wo_h = din("woT", [D, D])
    wr_h = din("wrT", [D, D]); wk2_h = din("wk2T", [D, D]); wv2_h = din("wv2T", [D, D])
    konst_h = din("konst", [128, 4])
    bq_h = din("bq2", [128, NCH]); bk_h = din("bk2", [128, NCH])
    bv_h = din("bv2", [128, NCH]); bo_h = din("bo2", [128, NCH])
    mur_h = din("mur2", [128, NCH]); muk_h = din("muk2", [128, NCH])
    gates_h = din("g64", [128, H])
    bv64_h = din("bv64", [128, H])
    strips_h = din("strips", [H, 128, STRIP_W])
    ident_h = din("ident", [128, 128], BF16)
    out_h = nc.dram_tensor("out", [B_CORE, T, D], F32, kind="ExternalOutput").ap()

    dbg_h = {}
    if debug:
        for nm, shp, dty in [
            ("d_xnT", [B_CORE, 128, NCH, T], F32),
            ("d_Q", [B_CORE, 128, NCH, T], F32),
            ("d_K", [B_CORE, 128, NCH, T], F32),
            ("d_V", [B_CORE, 128, NTT, D], F32),
            ("d_sb", [128, T], F32),
            ("d_e", [128, T], mybir.dt.bfloat16),
            ("d_c32", [128, 32], F32),
            ("d_attnT", [B_CORE, 128, NCH, T], F32),
            ("d_xtile", [128, 1024], F32),
            ("d_stats", [128, 16], F32),
            ("d_xntok", [128, 1024], F32),
            ("d_outT", [B_CORE, 128, NCH, T], F32),
        ]:
            dbg_h[nm] = nc.dram_tensor(nm, shp, dty, kind="ExternalOutput").ap()

    p = Plan()

    import contextlib
    with contextlib.ExitStack() as _stk:
        def _sb(name, shape, dtype):
            return _stk.enter_context(nc.sbuf_tensor(name, shape, dtype))
        AC = _sb("AC", [128, NCH, T + 4], BF16)
        Qa = _sb("Qa", [128, NCH, T], BF16)
        Ka = _sb("Ka", [128, NCH, T], BF16)
        Va = _sb("Va", [128, NTT, D], BF16)
        Ua = _sb("Ua", [128, 8192], BF16)
        Ea = _sb("Ea", [128, 2048], F32)
        ef1 = _sb("ef1", [128, 2048], F32)
        e2f = _sb("e2f", [128, 2, 2048], F32)
        et0 = _sb("et0", [128, 2048], BF16)
        et1 = _sb("et1", [128, 2048], BF16)
        eTt0 = _sb("eTt0", [128, 16, 128], BF16)
        g3 = _sb("g3", [128, 496], F32)
        xntok = et0[:, 0:1024]
        xntok2 = et0[:, 1024:2048]
        wtmp = _sb("wtmp", [128, 512], BF16)
        identt = _sb("identt", [128, 128], BF16)
        c32t = _sb("c32", [128, 32], F32)
        bq2s = _sb("bq2s", [128, NCH], F32)
        bk2s = _sb("bk2s", [128, NCH], F32)
        bo2s = _sb("bo2s", [128, NCH], F32)
        mur2s = _sb("mur2s", [128, NCH], F32)
        muk2s = _sb("muk2s", [128, NCH], F32)
        gatess = _sb("gatess", [128, H], F32)
        bv64s = _sb("bv64s", [128, H], F32)
        gbvs = _sb("gbvs", [128, H], F32)
        statst = _sb("stats", [128, 32], F32)
        konsts = _sb("konsts", [128, 4], F32)
        ps6 = _stk.enter_context(nc.psum_tensor("ps6", [128, 6, 512], F32))
        pstr = _stk.enter_context(nc.psum_tensor("pstr", [128, 2, 1024], BF16))
        SV = _stk.enter_context(nc.semaphore("SV"))
        SS = _stk.enter_context(nc.semaphore("SS"))
        STm = _stk.enter_context(nc.semaphore("ST"))
        SGm = _stk.enter_context(nc.semaphore("SG"))
        SY = _stk.enter_context(nc.semaphore("SY"))
        dE0 = _stk.enter_context(nc.semaphore("dE0"))
        dE1 = _stk.enter_context(nc.semaphore("dE1"))
        dE2 = _stk.enter_context(nc.semaphore("dE2"))
        dE3 = _stk.enter_context(nc.semaphore("dE3"))
        dG3 = _stk.enter_context(nc.semaphore("dG3"))
        dCN = _stk.enter_context(nc.semaphore("dCN"))
        dSO = _stk.enter_context(nc.semaphore("dSO"))
        dSO1 = _stk.enter_context(nc.semaphore("dSO1"))
        block = _stk.enter_context(nc.Block())
        sems = {("E", "v"): SV, ("E", "s"): SS, ("E", "t"): STm,
                ("E", "g"): SGm, ("E", "y"): SY,
                "dE0": dE0, "dE1": dE1, "dE2": dE2, "dE3": dE3,
                "dG3": dG3, "dCN": dCN, "dSO": dSO, "dSO1": dSO1}

        nc.const_aps.register(F32, 0.0, konsts[:, 0:1])
        nc.const_aps.register(F32, 1.0, konsts[:, 1:2])
        nc.const_aps.register(F32, 1e-5, konsts[:, 2:3])
        nc.const_aps.register(F32, -1.0, konsts[:, 3:4])

        bAC = Buf("AC"); bQ = Buf("Q"); bK = Buf("K"); bV = Buf("V")
        bW = Buf("W"); bS0 = Buf("S0"); bS1 = Buf("S1")
        bE0 = Buf("E0"); bE1 = Buf("E1")
        bEFx = [Buf("ef0"), Buf("ef1")]
        efv = [Ea[:], ef1[:]]
        bE2Fx = [Buf("e2f0"), Buf("e2f1")]
        e2fv = [e2f[:, 0, :], e2f[:, 1, :]]
        bEx = [Buf("e0"), Buf("e1")]
        _bET = Buf("eT0")
        bETx = [_bET, _bET]
        etv = [et0[:], et1[:]]
        eTv = [eTt0, eTt0]
        bG3 = Buf("g3"); bXN = Buf("xntok"); bXN2 = Buf("xntok2")
        bWT = Buf("wtmp")
        bC32 = Buf("c32"); bE32 = Buf("e32")
        bPS = [Buf(f"ps{i}") for i in range(6)]
        bPSS = Buf("pss")
        bTR0 = Buf("tr0"); bTR1 = Buf("tr1")
        bCN = Buf("consts")
        bST = [Buf(f"st{i}") for i in range(32)]
        bGBV = Buf("gbv")

        W8 = Ua[:].rearrange("p (c n) -> p c n", n=1024)
        strip0 = Ua[:, 0:STRIP_W]
        strip1 = Ua[:, 4096:4096 + STRIP_W]
        psS_flat = ps6[:, 0:4, :].rearrange("p a b -> p (a b)")

        def ld(dst, src):
            return lambda e, d=dst, s=src: e.dma_start(out=d, in_=s)

        _prev = None
        for dst, src in [(konsts[:], konst_h[:, :]),
                         (bq2s[:], bq_h[:, :]), (bk2s[:], bk_h[:, :]),
                         (bo2s[:], bo_h[:, :]),
                         (mur2s[:], mur_h[:, :]), (muk2s[:], muk_h[:, :]),
                         (gatess[:], gates_h[:, :]), (bv64s[:], bv64_h[:, :]),
                         (identt[:], ident_h[:, :])]:
            _prev = p.op("y", ld(dst, src), writes=[bCN], dma="dCN",
                         extra_deps=[_prev] if _prev else ())
        del _prev

        p.op("v", lambda e: e.memset(AC[:, :, 0:4], 0.0), writes=[bAC])

        p.op("v", lambda e: e.tensor_tensor(
            gbvs[:], gatess[:], bv64s[:], ALU.mult),
            reads=[bCN], writes=[bGBV])

        def load_weight(wh):
            for blk in range(8):
                eslot = blk % 2
                ebuf = bE0 if eslot == 0 else bE1
                dst = Ea[:, 0:1024] if eslot == 0 else Ea[:, 1024:2048]
                p.op("y", ld(dst, wh[128 * blk:128 * blk + 128, :]),
                     writes=[ebuf], dma="dE0" if eslot == 0 else "dE1")
                p.op("s", lambda e, d=blk, src=dst: e.copy(W8[:, d, :], src),
                     reads=[ebuf], writes=[bW, bS0, bS1])

        def ln_pass(b, dst_buf, dst_view, xnbufs, xnts, dbg=None, p_=None,
                    dbg_bufs=None):
            st = statst
            for tt in range(NTT):
                eslot = tt % 2
                ebuf = xnbufs[2][eslot]
                xsl = Ea[:, 0:1024] if eslot == 0 else Ea[:, 1024:2048]
                p.op("y" if eslot == 0 else "g",
                     ld(xsl, x_h[b, 128 * tt:128 * tt + 128, :]),
                     writes=[ebuf], dma="dE0" if eslot == 0 else "dE2")
                so = 9 * (tt % 2)
                bsum, bsq, bnm, bvar, brstd, bnmr = (bST[so + i] for i in range(6))
                xnb = xnbufs[0] if tt % 2 == 0 else xnbufs[1]
                xnt = xnts[0] if tt % 2 == 0 else xnts[1]
                p.op("v", lambda e, x=xsl: e.reduce_sum(st[:, so + 0:so + 1], x, axis=AX.X),
                     reads=[ebuf], writes=[bsum])
                p.op("s", lambda e, x=xsl, o=xnt: e.activation(
                    o[:], x, AF.Square, accum_out=st[:, so + 1:so + 2]),
                    reads=[ebuf], writes=[bsq, xnb])
                p.op("v", lambda e: e.tensor_scalar(
                    st[:, so + 2:so + 3], st[:, so + 0:so + 1], -1.0 / D, None, ALU.mult),
                    reads=[bsum], writes=[bnm])
                p.op("v", lambda e: e.tensor_tensor(
                    st[:, so + 3:so + 4], st[:, so + 2:so + 3], st[:, so + 2:so + 3], ALU.mult),
                    reads=[bnm], writes=[bvar])
                p.op("v", lambda e: e.tensor_scalar(
                    st[:, so + 4:so + 5], st[:, so + 1:so + 2], 1.0 / D, None, ALU.mult,
                    accum_out=None), reads=[bsq], writes=[bvar])
                p.op("v", lambda e: e.tensor_tensor(
                    st[:, so + 5:so + 6], st[:, so + 4:so + 5], st[:, so + 3:so + 4], ALU.subtract),
                    reads=[bvar], writes=[bvar])
                p.op("s", lambda e: e.activation(
                    st[:, so + 6:so + 7], st[:, so + 5:so + 6], AF.Sqrt, bias=1e-5),
                    reads=[bvar], writes=[brstd])
                p.op("v", lambda e: e.reciprocal(st[:, so + 7:so + 8], st[:, so + 6:so + 7]),
                     reads=[brstd], writes=[brstd])
                p.op("v", lambda e: e.tensor_tensor(
                    st[:, so + 8:so + 9], st[:, so + 2:so + 3], st[:, so + 7:so + 8], ALU.mult),
                    reads=[bnm, brstd], writes=[bnmr])
                p.op("s", lambda e, x=xsl, o=xnt: e.activation(
                    o[:], x, AF.Identity, bias=st[:, so + 8:so + 9], scale=st[:, so + 7:so + 8]),
                    reads=[ebuf, bnmr, brstd], writes=[xnb])
                trb = bTR0 if tt % 2 == 0 else bTR1
                bank = 0 if tt % 2 == 0 else 1
                for ci in range(8):
                    p.op("t", lambda e, ci=ci, bank=bank, xnt=xnt: e.matmul(
                        pstr[:, bank, 128 * ci:128 * ci + 128],
                        xnt[:, 128 * ci:128 * ci + 128], identt[:],
                        is_transpose=True, start=(ci == 0), stop=(ci == 7),
                        skip_group_check=True),
                        reads=[xnb, bCN], writes=[trb])
                p.op("v", lambda e, tt=tt, bank=bank, dv=dst_view: e.tensor_copy(
                    dv[:, :, 4 + 128 * tt:4 + 128 * tt + 128],
                    pstr[:, bank, :].rearrange("p (c n) -> p c n", n=128)),
                    reads=[trb], writes=[dst_buf])
                if dbg is not None and tt == 0:
                    p.op("g", lambda e, x=xsl: e.dma_start(
                        out=dbg["d_xtile"][:, :], in_=x), reads=[dbg_bufs[0]],
                        dma="dSO")
                    p.op("g", lambda e: e.dma_start(
                        out=dbg["d_stats"][:, :], in_=st[:]),
                        reads=list(dbg_bufs[1][:9]), dma="dSO")
                    p.op("g", lambda e, o=xnt: e.dma_start(
                        out=dbg["d_xntok"][:, :], in_=o[:]),
                        reads=[dbg_bufs[2]], dma="dSO")

        for b in range(B_CORE):
            load_weight(wq_h)
            ln_pass(b, bAC, AC[:], (bXN, bXN2, (bE0, bE1)), (xntok[:], xntok2[:]),
                    dbg=(dbg_h if (debug and b == 0) else None), p_=p,
                    dbg_bufs=(bE0, bST, bXN))
            if debug:
                p.op("g", lambda e, b=b: e.dma_start(
                    out=dbg_h["d_xnT"][b], in_=AC[:, :, 4:4 + T]),
                    reads=[bAC], dma="dSO")

            xnT = AC[:]
            for wi_, (wh, dstv, dstb, bias_s, scale) in enumerate([
                (wq_h, Qa[:], bQ, bq2s, 0.125),
                (wk_h, Ka[:], bK, bk2s, None),
            ]):
                if wi_ > 0:
                    load_weight(wh)
                for oc in range(8):
                    for th in range(4):
                        pb = bPS[4 + (oc * 4 + th) % 2]
                        bank = 4 + (oc * 4 + th) % 2
                        for kc in range(8):
                            p.op("t", lambda e, kc=kc, oc=oc, th=th, bank=bank:
                                 e.matmul(ps6[:, bank, :],
                                          W8[:, kc, 128 * oc:128 * oc + 128],
                                          xnT[:, kc, 4 + 512 * th:4 + 512 * th + 512],
                                          start=(kc == 0), stop=(kc == 7)),
                                 reads=[bW, bAC], writes=[pb])
                        if scale is None:
                            p.op("v", lambda e, oc=oc, th=th, bank=bank, dv=dstv,
                                 bs=bias_s: e.tensor_scalar(
                                     dv[:, oc, 512 * th:512 * th + 512],
                                     ps6[:, bank, :], bs[:, oc:oc + 1], None,
                                     ALU.add),
                                 reads=[pb, bCN], writes=[dstb])
                        else:
                            p.op("v", lambda e, oc=oc, th=th, bank=bank, dv=dstv,
                                 bs=bias_s, sc=scale: e.tensor_scalar(
                                     dv[:, oc, 512 * th:512 * th + 512],
                                     ps6[:, bank, :], bs[:, oc:oc + 1], sc,
                                     ALU.add, ALU.mult),
                                 reads=[pb, bCN], writes=[dstb])
            load_weight(wv_h)
            for tt in range(NTT):
                for dh in range(2):
                    bank = 4 + (tt * 2 + dh) % 2
                    pb = bPS[bank]
                    for kc in range(8):
                        p.op("t", lambda e, kc=kc, tt=tt, dh=dh, bank=bank:
                             e.matmul(ps6[:, bank, :],
                                      xnT[:, kc, 4 + 128 * tt:4 + 128 * tt + 128],
                                      W8[:, kc, 512 * dh:512 * dh + 512],
                                      start=(kc == 0), stop=(kc == 7)),
                             reads=[bW, bAC], writes=[pb])
                    p.op("s", lambda e, tt=tt, dh=dh, bank=bank: e.copy(
                        Va[:, tt, 512 * dh:512 * dh + 512], ps6[:, bank, :]),
                        reads=[pb], writes=[bV])
            if debug:
                p.op("g", lambda e, b=b: e.dma_start(
                    out=dbg_h["d_Q"][b], in_=Qa[:]), reads=[bQ], dma="dSO")
                p.op("g", lambda e, b=b: e.dma_start(
                    out=dbg_h["d_K"][b], in_=Ka[:]), reads=[bK], dma="dSO")
                p.op("g", lambda e, b=b: e.dma_start(
                    out=dbg_h["d_V"][b], in_=Va[:]), reads=[bV], dma="dSO")

            attnT = AC[:]
            st = statst

            def load_strip(h):
                slot = h % 2
                sbuf = bS0 if slot == 0 else bS1
                sview = strip0 if slot == 0 else strip1
                for q4 in range(8):
                    w0 = 496 * q4
                    w1 = min(STRIP_W, w0 + 496)
                    p.op("y", lambda e, h=h, w0=w0, w1=w1: e.dma_start(
                        out=g3[:, 0:w1 - w0], in_=strips_h[h, :, w0:w1]),
                        writes=[bG3], dma="dG3")
                    p.op("s", lambda e, w0=w0, w1=w1, sv=sview: e.copy(
                        sv[:, w0:w1], g3[:, 0:w1 - w0]),
                        reads=[bG3], writes=[sbuf, bW])

            tiles = [(h, qt) for h in range(H) for qt in range(NTT)]

            def stage_a(i):
                h, qt = tiles[i]
                hc, hp = h // 2, h % 2
                sbuf = bS0 if h % 2 == 0 else bS1
                sview = strip0 if h % 2 == 0 else strip1
                pi = i % 2
                off = 1920 - 128 * qt
                for kc in range(4):
                    p.op("t", lambda e, kc=kc, qt=qt, hc=hc, hp=hp: e.matmul(
                        ps6[:, kc, :],
                        Qa[64 * hp:64 * hp + 64, hc, 128 * qt:128 * qt + 128],
                        Ka[64 * hp:64 * hp + 64, hc, 512 * kc:512 * kc + 512],
                        start=True, stop=False, skip_group_check=True),
                        reads=[bQ, bK], writes=[bPSS])
                for kc in range(4):
                    p.op("t", lambda e, kc=kc, off=off, sv=sview: e.matmul(
                        ps6[:, kc, :], identt[:],
                        sv[:, off + 512 * kc:off + 512 * kc + 512],
                        start=False, stop=True, skip_group_check=True),
                        reads=[sbuf, bCN], writes=[bPSS])
                wl = [bEFx[pi]] + ([bE0, bE1] if pi == 0 else [])
                q4 = i % 4
                p.op("s", lambda e, efp=efv[pi], q4=q4: e.activation(
                    efp, psS_flat, AF.Exp,
                    accum_out=st[:, 24 + q4:25 + q4]),
                    reads=[bPSS], writes=wl + [bST[24 + q4]])

            load_strip(0)
            stage_a(0)
            n_t = len(tiles)

            def rounds_a(i):
                pi = i % 2
                efp, e2fp = efv[pi], e2fv[pi]
                bE2Fp = bE2Fx[pi]
                rEF = [bEFx[pi]] + ([bE0, bE1] if pi == 0 else [])
                p.op("v", lambda e, efp=efp: e.max(out=c32t[:, 0:8], in_=efp),
                     reads=rEF, writes=[bC32])
                p.op("v", lambda e, efp=efp, e2fp=e2fp: e.match_replace(
                    out=e2fp, in_to_replace=c32t[:, 0:8],
                    in_values=efp, imm_value=0.0),
                    reads=[bC32] + rEF, writes=[bE2Fp])
                for rnd in range(1, 3):
                    p.op("v", lambda e, r=rnd, e2fp=e2fp: e.max(
                        out=c32t[:, 8 * r:8 * r + 8], in_=e2fp),
                        reads=[bE2Fp], writes=[bC32])
                    p.op("v", lambda e, r=rnd, e2fp=e2fp: e.match_replace(
                        out=e2fp, in_to_replace=c32t[:, 8 * r:8 * r + 8],
                        in_values=e2fp, imm_value=0.0),
                        reads=[bC32], writes=[bE2Fp])

            def rounds_b(i):
                pi = i % 2
                efp, e2fp = efv[pi], e2fv[pi]
                bE2Fp = bE2Fx[pi]
                rEF = [bEFx[pi]] + ([bE0, bE1] if pi == 0 else [])
                rEp = [bEx[pi]] + ([bXN, bXN2] if pi == 0 else [])
                ep = etv[pi]
                p.op("v", lambda e, e2fp=e2fp: e.max(
                    out=c32t[:, 24:32], in_=e2fp),
                    reads=[bE2Fp], writes=[bC32])
                p.op("v", lambda e, e2fp=e2fp: e.match_replace(
                    out=e2fp, in_to_replace=c32t[:, 24:32],
                    in_values=e2fp, imm_value=0.0),
                    reads=[bC32], writes=[bE2Fp])
                q4 = i % 4
                p.op("s", lambda e, ep=ep, e2fp=e2fp, q4=q4: e.activation(
                    ep, e2fp, AF.Identity, scale=-1.0,
                    accum_out=st[:, 28 + q4:29 + q4]),
                    reads=[bE2Fp], writes=rEp + [bST[28 + q4]])
                p.op("g", lambda e, ep=ep, efp=efp, e2fp=e2fp: e.tensor_tensor(
                    ep, efp, e2fp, ALU.subtract),
                    reads=[bE2Fp] + rEF, writes=rEp)

            def tail_zr(i):
                pi = i % 2
                q4 = i % 4
                p.op("s", lambda e, pi=pi, q4=q4: e.activation(
                    st[:, 18 + pi:19 + pi], st[:, 24 + q4:25 + q4],
                    AF.Identity, bias=st[:, 28 + q4:29 + q4]),
                    reads=[bST[24 + q4], bST[28 + q4]], writes=[bST[18 + pi]])
                p.op("v", lambda e, pi=pi: e.reciprocal(
                    st[:, 20 + pi:21 + pi], st[:, 18 + pi:19 + pi]),
                    reads=[bST[18 + pi]], writes=[bST[20 + pi]])

            def tail_scale(i):
                pi = i % 2
                rEp = [bEx[pi]] + ([bXN, bXN2] if pi == 0 else [])
                ep = etv[pi]
                p.op("v", lambda e, ep=ep, pi=pi: e.tensor_scalar(
                    ep, ep, st[:, 20 + pi:21 + pi], None, ALU.mult),
                    reads=[bST[20 + pi]] + rEp, writes=rEp)

            def tail_rest(i):
                h, qt = tiles[i]
                hc, hp = h // 2, h % 2
                pi = i % 2
                bEp, bETp = bEx[pi], bETx[pi]
                ep, eTp = etv[pi], eTv[pi]
                rEp = [bEp] + ([bXN, bXN2] if pi == 0 else [])
                if debug and h == 0 and qt == 0 and b == 0:
                    p.op("g", lambda e, ep=ep: e.dma_start(
                        out=dbg_h["d_e"][:, :], in_=ep),
                        reads=[bEp], dma="dSO")
                for g in range(2):
                    trb = bTR0 if g == 0 else bTR1
                    for j in range(8):
                        ci = 8 * g + j
                        p.op("t", lambda e, ci=ci, g=g, j=j, ep=ep: e.matmul(
                            pstr[:, g, 128 * j:128 * j + 128],
                            ep[:, 128 * ci:128 * ci + 128], identt[:],
                            is_transpose=True, start=(j == 0), stop=(j == 7),
                            skip_group_check=True),
                            reads=rEp + [bCN], writes=[trb])
                    p.op("s", lambda e, g=g, eTp=eTp: e.copy(
                        eTp[:, 8 * g:8 * g + 8, :],
                        pstr[:, g, :].rearrange("p (c n) -> p c n", n=128)),
                        reads=[trb], writes=[bETp])
                bank = 4 + pi
                pb = bPS[bank]
                for c in range(16):
                    p.op("t", lambda e, c=c, h=h, bank=bank, eTp=eTp: e.matmul(
                        ps6[0:64, bank, 0:128],
                        Va[:, c, 64 * h:64 * h + 64],
                        eTp[:, c, :],
                        start=(c == 0), stop=(c == 15)),
                        reads=[bV, bETp], writes=[pb])
                p.op("s", lambda e, h=h, hc=hc, hp=hp, qt=qt, bank=bank:
                     e.activation(
                         attnT[64 * hp:64 * hp + 64, hc,
                               4 + 128 * qt:4 + 128 * qt + 128],
                         ps6[0:64, bank, 0:128], AF.Identity,
                         scale=gatess[64 * hp:64 * hp + 64, h:h + 1],
                         bias=gbvs[64 * hp:64 * hp + 64, h:h + 1]),
                     reads=[pb, bCN, bGBV], writes=[bAC])

            for i, (h, qt) in enumerate(tiles):
                if qt == 0 and h + 1 < H:
                    load_strip(h + 1)
                rounds_a(i)
                if i >= 1:
                    tail_zr(i - 1)
                if i + 1 < n_t:
                    stage_a(i + 1)
                rounds_b(i)
                if debug and i == 0 and b == 0:
                    p.op("g", lambda e: e.dma_start(
                        out=dbg_h["d_c32"][:, :], in_=c32t[:]),
                        reads=[bC32], dma="dSO")
                if i >= 1:
                    tail_scale(i - 1)
                if i >= 1:
                    tail_rest(i - 1)
            tail_zr(n_t - 1)
            tail_scale(n_t - 1)
            tail_rest(n_t - 1)
            if debug:
                p.op("g", lambda e, b=b: e.dma_start(
                    out=dbg_h["d_attnT"][b], in_=attnT[:, :, 4:4 + T]),
                    reads=[bAC], dma="dSO")

            outT = Va[:].rearrange("p a b -> p (a b)").rearrange(
                "p (c n) -> p c n", n=T)
            bOT = bV
            load_weight(wo_h)
            for oc in range(8):
                for th in range(4):
                    bank = 4 + (oc * 4 + th) % 2
                    pb = bPS[bank]
                    for kc in range(8):
                        p.op("t", lambda e, kc=kc, oc=oc, th=th, bank=bank:
                             e.matmul(ps6[:, bank, :],
                                      W8[:, kc, 128 * oc:128 * oc + 128],
                                      attnT[:, kc, 4 + 512 * th:4 + 512 * th + 512],
                                      start=(kc == 0), stop=(kc == 7)),
                             reads=[bW, bAC], writes=[pb])
                    p.op("v", lambda e, oc=oc, th=th, bank=bank: e.tensor_scalar(
                        outT[:, oc, 512 * th:512 * th + 512],
                        ps6[:, bank, :], bo2s[:, oc:oc + 1], None, ALU.add),
                        reads=[pb, bCN], writes=[bOT])

            load_weight(wr_h)
            ln_pass(b, bAC, AC[:], (bXN, bXN2, (bE0, bE1)), (xntok[:], xntok2[:]))
            xnT2 = AC[:]
            xr = Qa[:]; bXR = bQ
            xk = Ka[:]; bXK = bK
            for ts_ in range(4):
                t0 = 4 + 512 * ts_
                for ci in range(8):
                    dsl = et0[:, 512 * (ci % 4):512 * (ci % 4) + 512]
                    p.op("v", lambda e, ci=ci, t0=t0, dsl=dsl: e.tensor_tensor(
                        dsl, xnT2[:, ci, t0:t0 + 512],
                        xnT2[:, ci, t0 - 1:t0 + 511], ALU.subtract),
                        reads=[bAC], writes=[bEx[0], bXN, bXN2])
                    p.op("v", lambda e, ci=ci, t0=t0, ts_=ts_, dsl=dsl:
                         e.scalar_tensor_tensor(
                             xr[:, ci, t0 - 4:t0 - 4 + 512], dsl,
                             mur2s[:, ci:ci + 1],
                             xnT2[:, ci, t0 - 1:t0 + 511], ALU.mult, ALU.add),
                         reads=[bEx[0], bAC, bCN], writes=[bXR])
                    p.op("v", lambda e, ci=ci, t0=t0, ts_=ts_, dsl=dsl:
                         e.scalar_tensor_tensor(
                             xk[:, ci, t0 - 4:t0 - 4 + 512], dsl,
                             muk2s[:, ci:ci + 1],
                             xnT2[:, ci, t0 - 1:t0 + 511], ALU.mult, ALU.add),
                         reads=[bEx[0], bAC, bCN], writes=[bXK])
            rT = AC[:]
            for oc in range(8):
                for th in range(4):
                    bank = 4 + (oc * 4 + th) % 2
                    pb = bPS[bank]
                    for kc in range(8):
                        p.op("t", lambda e, kc=kc, oc=oc, th=th, bank=bank:
                             e.matmul(ps6[:, bank, :],
                                      W8[:, kc, 128 * oc:128 * oc + 128],
                                      xr[:, kc, 512 * th:512 * th + 512],
                                      start=(kc == 0), stop=(kc == 7)),
                             reads=[bW, bXR], writes=[pb])
                    p.op("s", lambda e, oc=oc, th=th, bank=bank: e.activation(
                        rT[:, oc, 4 + 512 * th:4 + 512 * th + 512],
                        ps6[:, bank, :], AF.Sigmoid),
                        reads=[pb], writes=[bAC])
            load_weight(wk2_h)
            kaT = Qa[:]
            for oc in range(8):
                for th in range(4):
                    bank = 4 + (oc * 4 + th) % 2
                    pb = bPS[bank]
                    for kc in range(8):
                        p.op("t", lambda e, kc=kc, oc=oc, th=th, bank=bank:
                             e.matmul(ps6[:, bank, :],
                                      W8[:, kc, 128 * oc:128 * oc + 128],
                                      xk[:, kc, 512 * th:512 * th + 512],
                                      start=(kc == 0), stop=(kc == 7)),
                             reads=[bW, bXK], writes=[pb])
                    p.op("v", lambda e, oc=oc, th=th, bank=bank: e.tensor_scalar(
                        wtmp[:], ps6[:, bank, :], 0.0, None, ALU.max),
                        reads=[pb], writes=[bWT])
                    p.op("v", lambda e, oc=oc, th=th: e.tensor_tensor(
                        kaT[:, oc, 512 * th:512 * th + 512], wtmp[:], wtmp[:],
                        ALU.mult),
                        reads=[bWT], writes=[bXR])
            load_weight(wv2_h)
            for oc in range(8):
                for th in range(4):
                    bank = 4 + (oc * 4 + th) % 2
                    pb = bPS[bank]
                    for kc in range(8):
                        p.op("t", lambda e, kc=kc, oc=oc, th=th, bank=bank:
                             e.matmul(ps6[:, bank, :],
                                      W8[:, kc, 128 * oc:128 * oc + 128],
                                      kaT[:, kc, 512 * th:512 * th + 512],
                                      start=(kc == 0), stop=(kc == 7)),
                             reads=[bW, bXR], writes=[pb])
                    p.op("v", lambda e, oc=oc, th=th, bank=bank: e.tensor_tensor(
                        wtmp[:], ps6[:, bank, :],
                        rT[:, oc, 4 + 512 * th:4 + 512 * th + 512], ALU.mult),
                        reads=[pb, bAC], writes=[bWT])
                    p.op("v", lambda e, oc=oc, th=th: e.tensor_tensor(
                        outT[:, oc, 512 * th:512 * th + 512],
                        outT[:, oc, 512 * th:512 * th + 512], wtmp[:], ALU.add),
                        reads=[bWT], writes=[bOT])
            if debug:
                p.op("g", lambda e, b=b: e.dma_start(
                    out=dbg_h["d_outT"][b], in_=outT[:]), reads=[bOT], dma="dSO")

            for tt in range(NTT):
                eslot = tt % 2
                ebuf = bE0 if eslot == 0 else bE1
                xsl = Ea[:, 0:1024] if eslot == 0 else Ea[:, 1024:2048]
                p.op("y", ld(xsl, x_h[b, 128 * tt:128 * tt + 128, :]),
                     writes=[ebuf], dma="dE0" if eslot == 0 else "dE1")
                bank = tt % 2
                trb = bTR0 if bank == 0 else bTR1
                for oc in range(8):
                    p.op("t", lambda e, oc=oc, tt=tt, bank=bank: e.matmul(
                        pstr[:, bank, 128 * oc:128 * oc + 128],
                        outT[:, oc, 128 * tt:128 * tt + 128], identt[:],
                        is_transpose=True, start=(oc == 0), stop=(oc == 7),
                        skip_group_check=True),
                        reads=[bOT, bCN], writes=[trb])
                p.op("v", lambda e, xsl=xsl, bank=bank: e.tensor_tensor(
                    xsl, xsl, pstr[:, bank, :], ALU.add),
                    reads=[ebuf, trb], writes=[ebuf])
                it = p.op("g", lambda e, b=b, tt=tt, xsl=xsl: e.dma_start(
                    out=out_h[b, 128 * tt:128 * tt + 128, :], in_=xsl),
                    reads=[ebuf], dma="dSO" if tt % 2 == 0 else "dSO1")
                p.store_items.append(it)

        cnt = p.finalize()
        n_store = cnt.get("dSO", 0)
        n_store1 = cnt.get("dSO1", 0)

        def emit(block_eng, eng_key, eng_obj_getter=None):
            def body(eng):
                for it in p.items:
                    if it.eng != eng_key:
                        continue
                    for key, v in it.waits:
                        eng.wait_ge(sems[key], v)
                    ins = it.fn(eng)
                    if it.sig:
                        key = it.dma if it.dma else ("E", it.eng)
                        ins.then_inc(sems[key], 16 if it.dma else 1)
                if eng_key == "g":
                    eng.wait_ge(sems["dSO"], 16 * n_store)
                    eng.wait_ge(sems["dSO1"], 16 * n_store1)
            return body

        block.sync(emit(block, "y"))
        block.vector(emit(block, "v"))
        block.scalar(emit(block, "s"))
        block.tensor(emit(block, "t"))
        block.gpsimd(emit(block, "g"))

    return nc


def _stage_inputs(L0_attn_x, Wq, bq, Wk, bk, Wv, bv, Wo, bo, head_gates,
                  rel_bias, W_r, W_k2, W_v2, mu_r, mu_k):
    f32 = np.float32
    def feat(v):
        return np.ascontiguousarray(v.reshape(NCH, 128).T.astype(f32))
    common = {
        "wqT": np.ascontiguousarray(Wq.T.astype(f32)),
        "wkT": np.ascontiguousarray(Wk.T.astype(f32)),
        "wvT": np.ascontiguousarray(Wv.T.astype(f32)),
        "woT": np.ascontiguousarray(Wo.T.astype(f32)),
        "wrT": np.ascontiguousarray(W_r.T.astype(f32)),
        "wk2T": np.ascontiguousarray(W_k2.T.astype(f32)),
        "wv2T": np.ascontiguousarray(W_v2.T.astype(f32)),
        "bq2": feat(bq), "bk2": feat(bk), "bv2": feat(bv), "bo2": feat(bo),
        "mur2": feat(mu_r), "muk2": feat(mu_k),
        "g64": np.ascontiguousarray(
            np.repeat(head_gates.astype(f32)[None, :], 128, axis=0)),
        "bv64": np.ascontiguousarray(
            np.tile(bv.astype(f32).reshape(H, 64).T, (2, 1))),
        "ident": np.eye(128, dtype=ml_dtypes.bfloat16),
        "konst": np.tile(np.array([0.0, 1.0, 1e-5, -1.0], dtype=f32), (128, 1)),
    }
    cc = (np.arange(STRIP_W)[None, :] - np.arange(128)[:, None] - 1409)
    idx = np.clip(cc, 0, 2 * 512 - 2)
    common["strips"] = np.ascontiguousarray(
        rel_bias.astype(f32)[idx, :].transpose(2, 0, 1))
    x = np.asarray(L0_attn_x, dtype=f32)
    in_maps = []
    for c in range(NCORES):
        m = dict(common)
        m["x"] = np.ascontiguousarray(x[B_CORE * c:B_CORE * (c + 1)])
        in_maps.append(m)
    return in_maps


def kernel(L0_attn_x, L0_attn_Wq, L0_attn_bq, L0_attn_Wk, L0_attn_bk,
           L0_attn_Wv, L0_attn_bv, L0_attn_Wo, L0_attn_bo,
           L0_attn_head_gates, L0_attn_rel_bias,
           L0_ffn_W_r, L0_ffn_W_k, L0_ffn_W_v, L0_ffn_mu_r, L0_ffn_mu_k):
    in_maps = _stage_inputs(
        np.asarray(L0_attn_x), np.asarray(L0_attn_Wq), np.asarray(L0_attn_bq),
        np.asarray(L0_attn_Wk), np.asarray(L0_attn_bk),
        np.asarray(L0_attn_Wv), np.asarray(L0_attn_bv),
        np.asarray(L0_attn_Wo), np.asarray(L0_attn_bo),
        np.asarray(L0_attn_head_gates), np.asarray(L0_attn_rel_bias),
        np.asarray(L0_ffn_W_r), np.asarray(L0_ffn_W_k), np.asarray(L0_ffn_W_v),
        np.asarray(L0_ffn_mu_r), np.asarray(L0_ffn_mu_k))
    nc = build_program(debug=False)
    res = run_bass_kernel_spmd(nc, in_maps, core_ids=list(range(NCORES)))
    out = np.concatenate([res.results[c]["out"] for c in range(NCORES)], axis=0)
    return out.astype(np.float32)


# version 26
# speedup vs baseline: 1.0073x; 1.0073x over previous
import numpy as np
import ml_dtypes
import concourse.bass as bass
import concourse.mybir as mybir
from concourse.bass_utils import run_bass_kernel_spmd

F32 = mybir.dt.float32
BF16 = mybir.dt.bfloat16
AF = mybir.ActivationFunctionType
ALU = mybir.AluOpType
AX = mybir.AxisListType

NCORES = 8
B_CORE = 2
T = 2048
D = 1024
H = 16
HD = 64
TOPK = 32
NCH = 8
NTT = 16
STRIP_W = 3968


class _It:
    __slots__ = ("eng", "fn", "deps", "dma", "sig", "val", "waits", "idx")

    def __init__(self, eng, fn, deps, dma):
        self.eng = eng
        self.fn = fn
        self.deps = deps
        self.dma = dma
        self.sig = False
        self.val = 0
        self.waits = ()
        self.idx = -1


class Buf:
    __slots__ = ("name", "w", "rs")

    def __init__(self, name):
        self.name = name
        self.w = None
        self.rs = {}


class Plan:
    def __init__(self):
        self.items = []
        self.fence_deps = []
        self.fence_new = set()
        self.store_items = []

    def op(self, eng, fn, reads=(), writes=(), dma=None, extra_deps=(),
           soft_reads=()):
        deps = []
        soft = []
        for b in reads:
            if b.w is not None:
                deps.append(b.w)
        for b in soft_reads:
            if b.w is not None:
                if b.w.eng == eng and b.w.dma is None:
                    soft.append(b.w)
                else:
                    deps.append(b.w)
        for b in writes:
            deps.extend(b.rs.values())
            if b.w is not None:
                deps.append(b.w)
        deps.extend(extra_deps)
        if eng in self.fence_new:
            deps.extend(self.fence_deps)
            self.fence_new.discard(eng)
        it = _It(eng, fn, deps, dma)
        it.idx = len(self.items)
        self.items.append(it)
        for b in reads:
            b.rs[eng] = it
        for b in soft_reads:
            b.rs[eng] = it
        for b in writes:
            b.w = it
            b.rs = {}
        return it

    def fence(self):
        last = {}
        for it in self.items:
            last[it.eng] = it
        self.fence_deps = list(last.values())
        self.fence_new = {"v", "s", "t", "g", "y"}

    def finalize(self):
        for it in self.items:
            if it.dma in ("dSO", "dSO1", "dSO2", "dSO3"):
                it.sig = True
            for d in it.deps:
                if d is it:
                    continue
                if d.eng == "t" and it.eng == "t" and d.dma is None:
                    continue
                d.sig = True
        cnt = {}
        for it in self.items:
            if it.sig:
                key = it.dma if it.dma else ("E", it.eng)
                cnt[key] = cnt.get(key, 0) + 1
                it.val = cnt[key] * (16 if it.dma else 1)
        known = {}
        for it in self.items:
            k = known.setdefault(it.eng, {})
            ws = {}
            for d in it.deps:
                if d is it:
                    continue
                if d.eng == "t" and it.eng == "t" and d.dma is None:
                    continue
                if not d.sig:
                    continue
                key = d.dma if d.dma else ("E", d.eng)
                if ws.get(key, 0) < d.val:
                    ws[key] = d.val
            final = []
            for key, v in ws.items():
                if k.get(key, 0) < v:
                    k[key] = v
                    final.append((key, v))
            it.waits = tuple(final)
        return cnt


def build_program(debug=False):
    nc = bass.Bass("TRN2", target_bir_lowering=False, debug=False)

    dt = {}
    def din(name, shape, dtype=F32):
        dt[name] = nc.dram_tensor(name, shape, dtype, kind="ExternalInput").ap()
        return dt[name]

    x_h = din("x", [B_CORE, T, D])
    wq_h = din("wqT", [D, D]); wk_h = din("wkT", [D, D]); wv_h = din("wvT", [D, D])
    wo_h = din("woT", [D, D])
    wr_h = din("wrT", [D, D]); wk2_h = din("wk2T", [D, D]); wv2_h = din("wv2T", [D, D])
    konst_h = din("konst", [128, 4])
    bq_h = din("bq2", [128, NCH]); bk_h = din("bk2", [128, NCH])
    bv_h = din("bv2", [128, NCH]); bo_h = din("bo2", [128, NCH])
    mur_h = din("mur2", [128, NCH]); muk_h = din("muk2", [128, NCH])
    gates_h = din("g64", [128, H])
    bv64_h = din("bv64", [128, H])
    strips_h = din("strips", [H, 128, STRIP_W])
    ident_h = din("ident", [128, 128], BF16)
    out_h = nc.dram_tensor("out", [B_CORE, T, D], F32, kind="ExternalOutput").ap()

    dbg_h = {}
    if debug:
        for nm, shp, dty in [
            ("d_xnT", [B_CORE, 128, NCH, T], F32),
            ("d_Q", [B_CORE, 128, NCH, T], F32),
            ("d_K", [B_CORE, 128, NCH, T], F32),
            ("d_V", [B_CORE, 128, NTT, D], F32),
            ("d_sb", [128, T], F32),
            ("d_e", [128, T], mybir.dt.bfloat16),
            ("d_c32", [128, 32], F32),
            ("d_attnT", [B_CORE, 128, NCH, T], F32),
            ("d_xtile", [128, 1024], F32),
            ("d_stats", [128, 16], F32),
            ("d_xntok", [128, 1024], F32),
            ("d_outT", [B_CORE, 128, NCH, T], F32),
        ]:
            dbg_h[nm] = nc.dram_tensor(nm, shp, dty, kind="ExternalOutput").ap()

    p = Plan()

    import contextlib
    with contextlib.ExitStack() as _stk:
        def _sb(name, shape, dtype):
            return _stk.enter_context(nc.sbuf_tensor(name, shape, dtype))
        AC = _sb("AC", [128, NCH, T + 4], BF16)
        Qa = _sb("Qa", [128, NCH, T], BF16)
        Ka = _sb("Ka", [128, NCH, T], BF16)
        Va = _sb("Va", [128, NTT, D], BF16)
        Ua = _sb("Ua", [128, 8192], BF16)
        Ea = _sb("Ea", [128, 2048], F32)
        ef1 = _sb("ef1", [128, 2048], F32)
        e2f = _sb("e2f", [128, 2, 2048], F32)
        et0 = _sb("et0", [128, 2048], BF16)
        et1 = _sb("et1", [128, 2048], BF16)
        eTt0 = _sb("eTt0", [128, 16, 128], BF16)
        g3 = _sb("g3", [128, 496], F32)
        xntok = et0[:, 0:1024]
        xntok2 = et0[:, 1024:2048]
        wtmp = _sb("wtmp", [128, 512], BF16)
        identt = _sb("identt", [128, 128], BF16)
        c32t = _sb("c32", [128, 32], F32)
        bq2s = _sb("bq2s", [128, NCH], F32)
        bk2s = _sb("bk2s", [128, NCH], F32)
        bo2s = _sb("bo2s", [128, NCH], F32)
        mur2s = _sb("mur2s", [128, NCH], F32)
        muk2s = _sb("muk2s", [128, NCH], F32)
        gatess = _sb("gatess", [128, H], F32)
        bv64s = _sb("bv64s", [128, H], F32)
        gbvs = _sb("gbvs", [128, H], F32)
        statst = _sb("stats", [128, 32], F32)
        konsts = _sb("konsts", [128, 4], F32)
        ps6 = _stk.enter_context(nc.psum_tensor("ps6", [128, 6, 512], F32))
        pstr = _stk.enter_context(nc.psum_tensor("pstr", [128, 2, 1024], BF16))
        SV = _stk.enter_context(nc.semaphore("SV"))
        SS = _stk.enter_context(nc.semaphore("SS"))
        STm = _stk.enter_context(nc.semaphore("ST"))
        SGm = _stk.enter_context(nc.semaphore("SG"))
        SY = _stk.enter_context(nc.semaphore("SY"))
        dE0 = _stk.enter_context(nc.semaphore("dE0"))
        dE1 = _stk.enter_context(nc.semaphore("dE1"))
        dE2 = _stk.enter_context(nc.semaphore("dE2"))
        dE3 = _stk.enter_context(nc.semaphore("dE3"))
        dG3 = _stk.enter_context(nc.semaphore("dG3"))
        dCN = _stk.enter_context(nc.semaphore("dCN"))
        dSO = _stk.enter_context(nc.semaphore("dSO"))
        dSO1 = _stk.enter_context(nc.semaphore("dSO1"))
        dE4 = _stk.enter_context(nc.semaphore("dE4"))
        dE5 = _stk.enter_context(nc.semaphore("dE5"))
        dSO2 = _stk.enter_context(nc.semaphore("dSO2"))
        dSO3 = _stk.enter_context(nc.semaphore("dSO3"))
        block = _stk.enter_context(nc.Block())
        sems = {("E", "v"): SV, ("E", "s"): SS, ("E", "t"): STm,
                ("E", "g"): SGm, ("E", "y"): SY,
                "dE0": dE0, "dE1": dE1, "dE2": dE2, "dE3": dE3,
                "dG3": dG3, "dCN": dCN, "dSO": dSO, "dSO1": dSO1,
                "dE4": dE4, "dE5": dE5, "dSO2": dSO2, "dSO3": dSO3}

        nc.const_aps.register(F32, 0.0, konsts[:, 0:1])
        nc.const_aps.register(F32, 1.0, konsts[:, 1:2])
        nc.const_aps.register(F32, 1e-5, konsts[:, 2:3])

        bAC = Buf("AC"); bQ = Buf("Q"); bK = Buf("K"); bV = Buf("V")
        bW = Buf("W"); bS0 = Buf("S0"); bS1 = Buf("S1")
        bE0 = Buf("E0"); bE1 = Buf("E1")
        bEFx = [Buf("ef0"), Buf("ef1")]
        efv = [Ea[:], ef1[:]]
        bE2Fx = [Buf("e2f0"), Buf("e2f1")]
        e2fv = [e2f[:, 0, :], e2f[:, 1, :]]
        bEx = [Buf("e0"), Buf("e1")]
        _bET = Buf("eT0")
        bETx = [_bET, _bET]
        etv = [et0[:], et1[:]]
        eTv = [eTt0, eTt0]
        bG3 = Buf("g3"); bXN = Buf("xntok"); bXN2 = Buf("xntok2")
        bWT = Buf("wtmp")
        bC32 = Buf("c32"); bE32 = Buf("e32")
        bPS = [Buf(f"ps{i}") for i in range(6)]
        bPSS = Buf("pss")
        bTR0 = Buf("tr0"); bTR1 = Buf("tr1")
        bCN = Buf("consts")
        bST = [Buf(f"st{i}") for i in range(32)]
        bGBV = Buf("gbv")

        W8 = Ua[:].rearrange("p (c n) -> p c n", n=1024)
        strip0 = Ua[:, 0:STRIP_W]
        strip1 = Ua[:, 4096:4096 + STRIP_W]
        psS_flat = ps6[:, 0:4, :].rearrange("p a b -> p (a b)")

        def ld(dst, src):
            return lambda e, d=dst, s=src: e.dma_start(out=d, in_=s)

        _prev = None
        for dst, src in [(konsts[:], konst_h[:, :]),
                         (bq2s[:], bq_h[:, :]), (bk2s[:], bk_h[:, :]),
                         (bo2s[:], bo_h[:, :]),
                         (mur2s[:], mur_h[:, :]), (muk2s[:], muk_h[:, :]),
                         (gatess[:], gates_h[:, :]), (bv64s[:], bv64_h[:, :]),
                         (identt[:], ident_h[:, :])]:
            _prev = p.op("y", ld(dst, src), writes=[bCN], dma="dCN",
                         extra_deps=[_prev] if _prev else ())
        del _prev

        p.op("v", lambda e: e.memset(AC[:, :, 0:4], 0.0), writes=[bAC])

        p.op("v", lambda e: e.tensor_tensor(
            gbvs[:], gatess[:], bv64s[:], ALU.mult),
            reads=[bCN], writes=[bGBV])

        def load_weight(wh):
            for blk in range(8):
                eslot = blk % 2
                ebuf = bE0 if eslot == 0 else bE1
                dst = Ea[:, 0:1024] if eslot == 0 else Ea[:, 1024:2048]
                p.op("y", ld(dst, wh[128 * blk:128 * blk + 128, :]),
                     writes=[ebuf], dma="dE0" if eslot == 0 else "dE1")
                p.op("s", lambda e, d=blk, src=dst: e.copy(W8[:, d, :], src),
                     reads=[ebuf], writes=[bW, bS0, bS1])

        def ln_pass(b, dst_buf, dst_view, xnbufs, xnts, dbg=None, p_=None,
                    dbg_bufs=None):
            st = statst
            for tt in range(NTT):
                eslot = tt % 2
                ebuf = xnbufs[2][eslot]
                xsl = Ea[:, 0:1024] if eslot == 0 else Ea[:, 1024:2048]
                p.op("y" if eslot == 0 else "g",
                     ld(xsl, x_h[b, 128 * tt:128 * tt + 128, :]),
                     writes=[ebuf], dma="dE0" if eslot == 0 else "dE2")
                so = 9 * (tt % 2)
                bsum, bsq, bnm, bvar, brstd, bnmr = (bST[so + i] for i in range(6))
                xnb = xnbufs[0] if tt % 2 == 0 else xnbufs[1]
                xnt = xnts[0] if tt % 2 == 0 else xnts[1]
                p.op("v", lambda e, x=xsl: e.reduce_sum(st[:, so + 0:so + 1], x, axis=AX.X),
                     reads=[ebuf], writes=[bsum])
                p.op("s", lambda e, x=xsl, o=xnt: e.activation(
                    o[:], x, AF.Square, accum_out=st[:, so + 1:so + 2]),
                    reads=[ebuf], writes=[bsq, xnb])
                p.op("v", lambda e: e.tensor_scalar(
                    st[:, so + 2:so + 3], st[:, so + 0:so + 1], -1.0 / D, None, ALU.mult),
                    reads=[bsum], writes=[bnm])
                p.op("v", lambda e: e.tensor_tensor(
                    st[:, so + 3:so + 4], st[:, so + 2:so + 3], st[:, so + 2:so + 3], ALU.mult),
                    reads=[bnm], writes=[bvar])
                p.op("v", lambda e: e.tensor_scalar(
                    st[:, so + 4:so + 5], st[:, so + 1:so + 2], 1.0 / D, None, ALU.mult,
                    accum_out=None), reads=[bsq], writes=[bvar])
                p.op("v", lambda e: e.tensor_tensor(
                    st[:, so + 5:so + 6], st[:, so + 4:so + 5], st[:, so + 3:so + 4], ALU.subtract),
                    reads=[bvar], writes=[bvar])
                p.op("s", lambda e: e.activation(
                    st[:, so + 6:so + 7], st[:, so + 5:so + 6], AF.Sqrt, bias=1e-5),
                    reads=[bvar], writes=[brstd])
                p.op("v", lambda e: e.reciprocal(st[:, so + 7:so + 8], st[:, so + 6:so + 7]),
                     reads=[brstd], writes=[brstd])
                p.op("v", lambda e: e.tensor_tensor(
                    st[:, so + 8:so + 9], st[:, so + 2:so + 3], st[:, so + 7:so + 8], ALU.mult),
                    reads=[bnm, brstd], writes=[bnmr])
                p.op("s", lambda e, x=xsl, o=xnt: e.activation(
                    o[:], x, AF.Identity, bias=st[:, so + 8:so + 9], scale=st[:, so + 7:so + 8]),
                    reads=[ebuf, bnmr, brstd], writes=[xnb])
                trb = bTR0 if tt % 2 == 0 else bTR1
                bank = 0 if tt % 2 == 0 else 1
                for ci in range(8):
                    p.op("t", lambda e, ci=ci, bank=bank, xnt=xnt: e.matmul(
                        pstr[:, bank, 128 * ci:128 * ci + 128],
                        xnt[:, 128 * ci:128 * ci + 128], identt[:],
                        is_transpose=True, start=(ci == 0), stop=(ci == 7),
                        skip_group_check=True),
                        reads=[xnb, bCN], writes=[trb])
                p.op("v", lambda e, tt=tt, bank=bank, dv=dst_view: e.tensor_copy(
                    dv[:, :, 4 + 128 * tt:4 + 128 * tt + 128],
                    pstr[:, bank, :].rearrange("p (c n) -> p c n", n=128)),
                    reads=[trb], writes=[dst_buf])
                if dbg is not None and tt == 0:
                    p.op("g", lambda e, x=xsl: e.dma_start(
                        out=dbg["d_xtile"][:, :], in_=x), reads=[dbg_bufs[0]],
                        dma="dSO")
                    p.op("g", lambda e: e.dma_start(
                        out=dbg["d_stats"][:, :], in_=st[:]),
                        reads=list(dbg_bufs[1][:9]), dma="dSO")
                    p.op("g", lambda e, o=xnt: e.dma_start(
                        out=dbg["d_xntok"][:, :], in_=o[:]),
                        reads=[dbg_bufs[2]], dma="dSO")

        for b in range(B_CORE):
            load_weight(wq_h)
            ln_pass(b, bAC, AC[:], (bXN, bXN2, (bE0, bE1)), (xntok[:], xntok2[:]),
                    dbg=(dbg_h if (debug and b == 0) else None), p_=p,
                    dbg_bufs=(bE0, bST, bXN))
            if debug:
                p.op("g", lambda e, b=b: e.dma_start(
                    out=dbg_h["d_xnT"][b], in_=AC[:, :, 4:4 + T]),
                    reads=[bAC], dma="dSO")

            xnT = AC[:]
            for wi_, (wh, dstv, dstb, bias_s, scale) in enumerate([
                (wq_h, Qa[:], bQ, bq2s, 0.125),
                (wk_h, Ka[:], bK, bk2s, None),
            ]):
                if wi_ > 0:
                    load_weight(wh)
                for oc in range(8):
                    for th in range(4):
                        pb = bPS[4 + (oc * 4 + th) % 2]
                        bank = 4 + (oc * 4 + th) % 2
                        for kc in range(8):
                            p.op("t", lambda e, kc=kc, oc=oc, th=th, bank=bank:
                                 e.matmul(ps6[:, bank, :],
                                          W8[:, kc, 128 * oc:128 * oc + 128],
                                          xnT[:, kc, 4 + 512 * th:4 + 512 * th + 512],
                                          start=(kc == 0), stop=(kc == 7)),
                                 reads=[bW, bAC], writes=[pb])
                        if scale is None:
                            p.op("v", lambda e, oc=oc, th=th, bank=bank, dv=dstv,
                                 bs=bias_s: e.tensor_scalar(
                                     dv[:, oc, 512 * th:512 * th + 512],
                                     ps6[:, bank, :], bs[:, oc:oc + 1], None,
                                     ALU.add),
                                 reads=[pb, bCN], writes=[dstb])
                        else:
                            p.op("v", lambda e, oc=oc, th=th, bank=bank, dv=dstv,
                                 bs=bias_s, sc=scale: e.tensor_scalar(
                                     dv[:, oc, 512 * th:512 * th + 512],
                                     ps6[:, bank, :], bs[:, oc:oc + 1], sc,
                                     ALU.add, ALU.mult),
                                 reads=[pb, bCN], writes=[dstb])
            load_weight(wv_h)
            for tt in range(NTT):
                for dh in range(2):
                    bank = 4 + (tt * 2 + dh) % 2
                    pb = bPS[bank]
                    for kc in range(8):
                        p.op("t", lambda e, kc=kc, tt=tt, dh=dh, bank=bank:
                             e.matmul(ps6[:, bank, :],
                                      xnT[:, kc, 4 + 128 * tt:4 + 128 * tt + 128],
                                      W8[:, kc, 512 * dh:512 * dh + 512],
                                      start=(kc == 0), stop=(kc == 7)),
                             reads=[bW, bAC], writes=[pb])
                    p.op("s", lambda e, tt=tt, dh=dh, bank=bank: e.copy(
                        Va[:, tt, 512 * dh:512 * dh + 512], ps6[:, bank, :]),
                        reads=[pb], writes=[bV])
            if debug:
                p.op("g", lambda e, b=b: e.dma_start(
                    out=dbg_h["d_Q"][b], in_=Qa[:]), reads=[bQ], dma="dSO")
                p.op("g", lambda e, b=b: e.dma_start(
                    out=dbg_h["d_K"][b], in_=Ka[:]), reads=[bK], dma="dSO")
                p.op("g", lambda e, b=b: e.dma_start(
                    out=dbg_h["d_V"][b], in_=Va[:]), reads=[bV], dma="dSO")

            attnT = AC[:]
            st = statst

            def load_strip(h):
                slot = h % 2
                sbuf = bS0 if slot == 0 else bS1
                sview = strip0 if slot == 0 else strip1
                for q4 in range(8):
                    w0 = 496 * q4
                    w1 = min(STRIP_W, w0 + 496)
                    p.op("y", lambda e, h=h, w0=w0, w1=w1: e.dma_start(
                        out=g3[:, 0:w1 - w0], in_=strips_h[h, :, w0:w1]),
                        writes=[bG3], dma="dG3")
                    p.op("s", lambda e, w0=w0, w1=w1, sv=sview: e.copy(
                        sv[:, w0:w1], g3[:, 0:w1 - w0]),
                        reads=[bG3], writes=[sbuf, bW])

            tiles = [(h, qt) for h in range(H) for qt in range(NTT)]

            def stage_a(i):
                h, qt = tiles[i]
                hc, hp = h // 2, h % 2
                sbuf = bS0 if h % 2 == 0 else bS1
                sview = strip0 if h % 2 == 0 else strip1
                pi = i % 2
                off = 1920 - 128 * qt
                for kc in range(4):
                    p.op("t", lambda e, kc=kc, qt=qt, hc=hc, hp=hp: e.matmul(
                        ps6[:, kc, :],
                        Qa[64 * hp:64 * hp + 64, hc, 128 * qt:128 * qt + 128],
                        Ka[64 * hp:64 * hp + 64, hc, 512 * kc:512 * kc + 512],
                        start=True, stop=False, skip_group_check=True),
                        reads=[bQ, bK], writes=[bPSS])
                for kc in range(4):
                    p.op("t", lambda e, kc=kc, off=off, sv=sview: e.matmul(
                        ps6[:, kc, :], identt[:],
                        sv[:, off + 512 * kc:off + 512 * kc + 512],
                        start=False, stop=True, skip_group_check=True),
                        reads=[sbuf, bCN], writes=[bPSS])
                wl = [bEFx[pi]] + ([bE0, bE1] if pi == 0 else [])
                q4 = i % 4
                p.op("s", lambda e, efp=efv[pi], q4=q4: e.activation(
                    efp, psS_flat, AF.Exp,
                    accum_out=st[:, 24 + q4:25 + q4]),
                    reads=[bPSS], writes=wl + [bST[24 + q4]])

            load_strip(0)
            stage_a(0)
            n_t = len(tiles)

            def rounds_a(i):
                pi = i % 2
                efp, e2fp = efv[pi], e2fv[pi]
                bE2Fp = bE2Fx[pi]
                rEF = [bEFx[pi]] + ([bE0, bE1] if pi == 0 else [])
                p.op("v", lambda e, efp=efp: e.max(out=c32t[:, 0:8], in_=efp),
                     reads=rEF, writes=[bC32])
                p.op("v", lambda e, efp=efp, e2fp=e2fp: e.match_replace(
                    out=e2fp, in_to_replace=c32t[:, 0:8],
                    in_values=efp, imm_value=0.0),
                    reads=[bC32] + rEF, writes=[bE2Fp])
                for rnd in range(1, 3):
                    p.op("v", lambda e, r=rnd, e2fp=e2fp: e.max(
                        out=c32t[:, 8 * r:8 * r + 8], in_=e2fp),
                        reads=[bE2Fp], writes=[bC32])
                    p.op("v", lambda e, r=rnd, e2fp=e2fp: e.match_replace(
                        out=e2fp, in_to_replace=c32t[:, 8 * r:8 * r + 8],
                        in_values=e2fp, imm_value=0.0),
                        reads=[bC32], writes=[bE2Fp])

            def rounds_b(i):
                pi = i % 2
                efp, e2fp = efv[pi], e2fv[pi]
                bE2Fp = bE2Fx[pi]
                rEF = [bEFx[pi]] + ([bE0, bE1] if pi == 0 else [])
                rEp = [bEx[pi]] + ([bXN, bXN2] if pi == 0 else [])
                ep = etv[pi]
                p.op("v", lambda e, e2fp=e2fp: e.max(
                    out=c32t[:, 24:32], in_=e2fp),
                    reads=[bE2Fp], writes=[bC32])
                p.op("v", lambda e, e2fp=e2fp: e.match_replace(
                    out=e2fp, in_to_replace=c32t[:, 24:32],
                    in_values=e2fp, imm_value=0.0),
                    reads=[bC32], writes=[bE2Fp])
                q4 = i % 4
                p.op("s", lambda e, ep=ep, e2fp=e2fp, q4=q4: e.activation(
                    ep, e2fp, AF.Identity,
                    accum_out=st[:, 28 + q4:29 + q4]),
                    reads=[bE2Fp], writes=rEp + [bST[28 + q4]])
                p.op("g", lambda e, ep=ep, efp=efp, e2fp=e2fp: e.tensor_tensor(
                    ep, efp, e2fp, ALU.subtract),
                    reads=[bE2Fp] + rEF, writes=rEp)

            def tail_zr(i):
                pi = i % 2
                q4 = i % 4
                p.op("v", lambda e, pi=pi, q4=q4: e.tensor_tensor(
                    st[:, 18 + pi:19 + pi], st[:, 24 + q4:25 + q4],
                    st[:, 28 + q4:29 + q4], ALU.subtract),
                    reads=[bST[24 + q4], bST[28 + q4]], writes=[bST[18 + pi]])
                p.op("v", lambda e, pi=pi: e.reciprocal(
                    st[:, 20 + pi:21 + pi], st[:, 18 + pi:19 + pi]),
                    reads=[bST[18 + pi]], writes=[bST[20 + pi]])

            def tail_scale(i):
                pi = i % 2
                rEp = [bEx[pi]] + ([bXN, bXN2] if pi == 0 else [])
                ep = etv[pi]
                p.op("v", lambda e, ep=ep, pi=pi: e.tensor_scalar(
                    ep, ep, st[:, 20 + pi:21 + pi], None, ALU.mult),
                    reads=[bST[20 + pi]] + rEp, writes=rEp)

            def tail_rest(i):
                h, qt = tiles[i]
                hc, hp = h // 2, h % 2
                pi = i % 2
                bEp, bETp = bEx[pi], bETx[pi]
                ep, eTp = etv[pi], eTv[pi]
                rEp = [bEp] + ([bXN, bXN2] if pi == 0 else [])
                if debug and h == 0 and qt == 0 and b == 0:
                    p.op("g", lambda e, ep=ep: e.dma_start(
                        out=dbg_h["d_e"][:, :], in_=ep),
                        reads=[bEp], dma="dSO")
                for g in range(2):
                    trb = bTR0 if g == 0 else bTR1
                    for j in range(8):
                        ci = 8 * g + j
                        p.op("t", lambda e, ci=ci, g=g, j=j, ep=ep: e.matmul(
                            pstr[:, g, 128 * j:128 * j + 128],
                            ep[:, 128 * ci:128 * ci + 128], identt[:],
                            is_transpose=True, start=(j == 0), stop=(j == 7),
                            skip_group_check=True),
                            reads=rEp + [bCN], writes=[trb])
                    p.op("s", lambda e, g=g, eTp=eTp: e.copy(
                        eTp[:, 8 * g:8 * g + 8, :],
                        pstr[:, g, :].rearrange("p (c n) -> p c n", n=128)),
                        reads=[trb], writes=[bETp])
                bank = 4 + pi
                pb = bPS[bank]
                for c in range(16):
                    p.op("t", lambda e, c=c, h=h, bank=bank, eTp=eTp: e.matmul(
                        ps6[0:64, bank, 0:128],
                        Va[:, c, 64 * h:64 * h + 64],
                        eTp[:, c, :],
                        start=(c == 0), stop=(c == 15)),
                        reads=[bV, bETp], writes=[pb])
                p.op("s", lambda e, h=h, hc=hc, hp=hp, qt=qt, bank=bank:
                     e.activation(
                         attnT[64 * hp:64 * hp + 64, hc,
                               4 + 128 * qt:4 + 128 * qt + 128],
                         ps6[0:64, bank, 0:128], AF.Identity,
                         scale=gatess[64 * hp:64 * hp + 64, h:h + 1],
                         bias=gbvs[64 * hp:64 * hp + 64, h:h + 1]),
                     reads=[pb, bCN, bGBV], writes=[bAC])

            for i, (h, qt) in enumerate(tiles):
                if qt == 0 and h + 1 < H:
                    load_strip(h + 1)
                rounds_a(i)
                if i >= 1:
                    tail_zr(i - 1)
                if i + 1 < n_t:
                    stage_a(i + 1)
                rounds_b(i)
                if debug and i == 0 and b == 0:
                    p.op("g", lambda e: e.dma_start(
                        out=dbg_h["d_c32"][:, :], in_=c32t[:]),
                        reads=[bC32], dma="dSO")
                if i >= 1:
                    tail_scale(i - 1)
                if i >= 1:
                    tail_rest(i - 1)
            tail_zr(n_t - 1)
            tail_scale(n_t - 1)
            tail_rest(n_t - 1)
            if debug:
                p.op("g", lambda e, b=b: e.dma_start(
                    out=dbg_h["d_attnT"][b], in_=attnT[:, :, 4:4 + T]),
                    reads=[bAC], dma="dSO")

            outT = Va[:].rearrange("p a b -> p (a b)").rearrange(
                "p (c n) -> p c n", n=T)
            bOT = bV
            load_weight(wo_h)
            for oc in range(8):
                for th in range(4):
                    bank = 4 + (oc * 4 + th) % 2
                    pb = bPS[bank]
                    for kc in range(8):
                        p.op("t", lambda e, kc=kc, oc=oc, th=th, bank=bank:
                             e.matmul(ps6[:, bank, :],
                                      W8[:, kc, 128 * oc:128 * oc + 128],
                                      attnT[:, kc, 4 + 512 * th:4 + 512 * th + 512],
                                      start=(kc == 0), stop=(kc == 7)),
                             reads=[bW, bAC], writes=[pb])
                    p.op("v", lambda e, oc=oc, th=th, bank=bank: e.tensor_scalar(
                        outT[:, oc, 512 * th:512 * th + 512],
                        ps6[:, bank, :], bo2s[:, oc:oc + 1], None, ALU.add),
                        reads=[pb, bCN], writes=[bOT])

            load_weight(wr_h)
            ln_pass(b, bAC, AC[:], (bXN, bXN2, (bE0, bE1)), (xntok[:], xntok2[:]))
            xnT2 = AC[:]
            xr = Qa[:]; bXR = bQ
            xk = Ka[:]; bXK = bK
            for ts_ in range(4):
                t0 = 4 + 512 * ts_
                for ci in range(8):
                    dsl = et0[:, 512 * (ci % 4):512 * (ci % 4) + 512]
                    p.op("v", lambda e, ci=ci, t0=t0, dsl=dsl: e.tensor_tensor(
                        dsl, xnT2[:, ci, t0:t0 + 512],
                        xnT2[:, ci, t0 - 1:t0 + 511], ALU.subtract),
                        reads=[bAC], writes=[bEx[0], bXN, bXN2])
                    p.op("v", lambda e, ci=ci, t0=t0, ts_=ts_, dsl=dsl:
                         e.scalar_tensor_tensor(
                             xr[:, ci, t0 - 4:t0 - 4 + 512], dsl,
                             mur2s[:, ci:ci + 1],
                             xnT2[:, ci, t0 - 1:t0 + 511], ALU.mult, ALU.add),
                         reads=[bEx[0], bAC, bCN], writes=[bXR])
                    p.op("v", lambda e, ci=ci, t0=t0, ts_=ts_, dsl=dsl:
                         e.scalar_tensor_tensor(
                             xk[:, ci, t0 - 4:t0 - 4 + 512], dsl,
                             muk2s[:, ci:ci + 1],
                             xnT2[:, ci, t0 - 1:t0 + 511], ALU.mult, ALU.add),
                         reads=[bEx[0], bAC, bCN], writes=[bXK])
            rT = AC[:]
            for oc in range(8):
                for th in range(4):
                    bank = 4 + (oc * 4 + th) % 2
                    pb = bPS[bank]
                    for kc in range(8):
                        p.op("t", lambda e, kc=kc, oc=oc, th=th, bank=bank:
                             e.matmul(ps6[:, bank, :],
                                      W8[:, kc, 128 * oc:128 * oc + 128],
                                      xr[:, kc, 512 * th:512 * th + 512],
                                      start=(kc == 0), stop=(kc == 7)),
                             reads=[bW, bXR], writes=[pb])
                    p.op("s", lambda e, oc=oc, th=th, bank=bank: e.activation(
                        rT[:, oc, 4 + 512 * th:4 + 512 * th + 512],
                        ps6[:, bank, :], AF.Sigmoid),
                        reads=[pb], writes=[bAC])
            load_weight(wk2_h)
            kaT = Qa[:]
            for oc in range(8):
                for th in range(4):
                    bank = 4 + (oc * 4 + th) % 2
                    pb = bPS[bank]
                    for kc in range(8):
                        p.op("t", lambda e, kc=kc, oc=oc, th=th, bank=bank:
                             e.matmul(ps6[:, bank, :],
                                      W8[:, kc, 128 * oc:128 * oc + 128],
                                      xk[:, kc, 512 * th:512 * th + 512],
                                      start=(kc == 0), stop=(kc == 7)),
                             reads=[bW, bXK], writes=[pb])
                    p.op("v", lambda e, oc=oc, th=th, bank=bank: e.tensor_scalar(
                        wtmp[:], ps6[:, bank, :], 0.0, None, ALU.max),
                        reads=[pb], writes=[bWT])
                    p.op("v", lambda e, oc=oc, th=th: e.tensor_tensor(
                        kaT[:, oc, 512 * th:512 * th + 512], wtmp[:], wtmp[:],
                        ALU.mult),
                        reads=[bWT], writes=[bXR])
            load_weight(wv2_h)
            for oc in range(8):
                for th in range(4):
                    bank = 4 + (oc * 4 + th) % 2
                    pb = bPS[bank]
                    for kc in range(8):
                        p.op("t", lambda e, kc=kc, oc=oc, th=th, bank=bank:
                             e.matmul(ps6[:, bank, :],
                                      W8[:, kc, 128 * oc:128 * oc + 128],
                                      kaT[:, kc, 512 * th:512 * th + 512],
                                      start=(kc == 0), stop=(kc == 7)),
                             reads=[bW, bXR], writes=[pb])
                    p.op("v", lambda e, oc=oc, th=th, bank=bank: e.tensor_tensor(
                        wtmp[:], ps6[:, bank, :],
                        rT[:, oc, 4 + 512 * th:4 + 512 * th + 512], ALU.mult),
                        reads=[pb, bAC], writes=[bWT])
                    p.op("v", lambda e, oc=oc, th=th: e.tensor_tensor(
                        outT[:, oc, 512 * th:512 * th + 512],
                        outT[:, oc, 512 * th:512 * th + 512], wtmp[:], ALU.add),
                        reads=[bWT], writes=[bOT])
            if debug:
                p.op("g", lambda e, b=b: e.dma_start(
                    out=dbg_h["d_outT"][b], in_=outT[:]), reads=[bOT], dma="dSO")

            x6v = [Ea[:, 0:1024], Ea[:, 1024:2048],
                   e2fv[0][:, 0:1024], e2fv[1][:, 0:1024]]
            x6b = [bE0, bE1, bE2Fx[0], bE2Fx[1]]
            for tt in range(NTT):
                eslot = tt % 4
                ebuf = x6b[eslot]
                xsl = x6v[eslot]
                p.op(("y", "g", "y", "g")[eslot],
                     ld(xsl, x_h[b, 128 * tt:128 * tt + 128, :]),
                     writes=[ebuf], dma=("dE0", "dE4", "dE3", "dE5")[eslot])
                bank = tt % 2
                trb = bTR0 if bank == 0 else bTR1
                for oc in range(8):
                    p.op("t", lambda e, oc=oc, tt=tt, bank=bank: e.matmul(
                        pstr[:, bank, 128 * oc:128 * oc + 128],
                        outT[:, oc, 128 * tt:128 * tt + 128], identt[:],
                        is_transpose=True, start=(oc == 0), stop=(oc == 7),
                        skip_group_check=True),
                        reads=[bOT, bCN], writes=[trb])
                p.op("v", lambda e, xsl=xsl, bank=bank: e.tensor_tensor(
                    xsl, xsl, pstr[:, bank, :], ALU.add),
                    reads=[ebuf, trb], writes=[ebuf])
                ssem = ("dSO", "dSO1", "dSO2", "dSO3")[eslot]
                it = p.op(("s", "s", "y", "s")[eslot],
                          lambda e, b=b, tt=tt, xsl=xsl: e.dma_start(
                    out=out_h[b, 128 * tt:128 * tt + 128, :], in_=xsl),
                    reads=[ebuf], dma=ssem)
                p.store_items.append(it)

        cnt = p.finalize()
        n_store = cnt.get("dSO", 0)
        n_store1 = cnt.get("dSO1", 0)
        n_g3 = cnt.get("dSO2", 0)
        n_cn = cnt.get("dSO3", 0)

        def emit(block_eng, eng_key, eng_obj_getter=None):
            def body(eng):
                for it in p.items:
                    if it.eng != eng_key:
                        continue
                    for key, v in it.waits:
                        eng.wait_ge(sems[key], v)
                    ins = it.fn(eng)
                    if it.sig:
                        key = it.dma if it.dma else ("E", it.eng)
                        ins.then_inc(sems[key], 16 if it.dma else 1)
                if eng_key == "g":
                    eng.wait_ge(sems["dSO"], 16 * n_store)
                    eng.wait_ge(sems["dSO1"], 16 * n_store1)
                    eng.wait_ge(sems["dSO2"], 16 * n_g3)
                    eng.wait_ge(sems["dSO3"], 16 * n_cn)
            return body

        block.sync(emit(block, "y"))
        block.vector(emit(block, "v"))
        block.scalar(emit(block, "s"))
        block.tensor(emit(block, "t"))
        block.gpsimd(emit(block, "g"))

    return nc


def _stage_inputs(L0_attn_x, Wq, bq, Wk, bk, Wv, bv, Wo, bo, head_gates,
                  rel_bias, W_r, W_k2, W_v2, mu_r, mu_k):
    f32 = np.float32
    def feat(v):
        return np.ascontiguousarray(v.reshape(NCH, 128).T.astype(f32))
    common = {
        "wqT": np.ascontiguousarray(Wq.T.astype(f32)),
        "wkT": np.ascontiguousarray(Wk.T.astype(f32)),
        "wvT": np.ascontiguousarray(Wv.T.astype(f32)),
        "woT": np.ascontiguousarray(Wo.T.astype(f32)),
        "wrT": np.ascontiguousarray(W_r.T.astype(f32)),
        "wk2T": np.ascontiguousarray(W_k2.T.astype(f32)),
        "wv2T": np.ascontiguousarray(W_v2.T.astype(f32)),
        "bq2": feat(bq), "bk2": feat(bk), "bv2": feat(bv), "bo2": feat(bo),
        "mur2": feat(mu_r), "muk2": feat(mu_k),
        "g64": np.ascontiguousarray(
            np.repeat(head_gates.astype(f32)[None, :], 128, axis=0)),
        "bv64": np.ascontiguousarray(
            np.tile(bv.astype(f32).reshape(H, 64).T, (2, 1))),
        "ident": np.eye(128, dtype=ml_dtypes.bfloat16),
        "konst": np.tile(np.array([0.0, 1.0, 1e-5, 0.0], dtype=f32), (128, 1)),
    }
    cc = (np.arange(STRIP_W)[None, :] - np.arange(128)[:, None] - 1409)
    idx = np.clip(cc, 0, 2 * 512 - 2)
    common["strips"] = np.ascontiguousarray(
        rel_bias.astype(f32)[idx, :].transpose(2, 0, 1))
    x = np.asarray(L0_attn_x, dtype=f32)
    in_maps = []
    for c in range(NCORES):
        m = dict(common)
        m["x"] = np.ascontiguousarray(x[B_CORE * c:B_CORE * (c + 1)])
        in_maps.append(m)
    return in_maps


def kernel(L0_attn_x, L0_attn_Wq, L0_attn_bq, L0_attn_Wk, L0_attn_bk,
           L0_attn_Wv, L0_attn_bv, L0_attn_Wo, L0_attn_bo,
           L0_attn_head_gates, L0_attn_rel_bias,
           L0_ffn_W_r, L0_ffn_W_k, L0_ffn_W_v, L0_ffn_mu_r, L0_ffn_mu_k):
    in_maps = _stage_inputs(
        np.asarray(L0_attn_x), np.asarray(L0_attn_Wq), np.asarray(L0_attn_bq),
        np.asarray(L0_attn_Wk), np.asarray(L0_attn_bk),
        np.asarray(L0_attn_Wv), np.asarray(L0_attn_bv),
        np.asarray(L0_attn_Wo), np.asarray(L0_attn_bo),
        np.asarray(L0_attn_head_gates), np.asarray(L0_attn_rel_bias),
        np.asarray(L0_ffn_W_r), np.asarray(L0_ffn_W_k), np.asarray(L0_ffn_W_v),
        np.asarray(L0_ffn_mu_r), np.asarray(L0_ffn_mu_k))
    nc = build_program(debug=False)
    res = run_bass_kernel_spmd(nc, in_maps, core_ids=list(range(NCORES)))
    out = np.concatenate([res.results[c]["out"] for c in range(NCORES)], axis=0)
    return out.astype(np.float32)


# version 27 (final)
# speedup vs baseline: 1.0118x; 1.0045x over previous
import numpy as np
import ml_dtypes
import concourse.bass as bass
import concourse.mybir as mybir
from concourse.bass_utils import run_bass_kernel_spmd

F32 = mybir.dt.float32
BF16 = mybir.dt.bfloat16
AF = mybir.ActivationFunctionType
ALU = mybir.AluOpType
AX = mybir.AxisListType

NCORES = 8
B_CORE = 2
T = 2048
D = 1024
H = 16
HD = 64
TOPK = 32
NCH = 8
NTT = 16
STRIP_W = 3968


class _It:
    __slots__ = ("eng", "fn", "deps", "dma", "sig", "val", "waits", "idx")

    def __init__(self, eng, fn, deps, dma):
        self.eng = eng
        self.fn = fn
        self.deps = deps
        self.dma = dma
        self.sig = False
        self.val = 0
        self.waits = ()
        self.idx = -1


class Buf:
    __slots__ = ("name", "w", "rs")

    def __init__(self, name):
        self.name = name
        self.w = None
        self.rs = {}


class Plan:
    def __init__(self):
        self.items = []
        self.fence_deps = []
        self.fence_new = set()
        self.store_items = []

    def op(self, eng, fn, reads=(), writes=(), dma=None, extra_deps=(),
           soft_reads=()):
        deps = []
        soft = []
        for b in reads:
            if b.w is not None:
                deps.append(b.w)
        for b in soft_reads:
            if b.w is not None:
                if b.w.eng == eng and b.w.dma is None:
                    soft.append(b.w)
                else:
                    deps.append(b.w)
        for b in writes:
            deps.extend(b.rs.values())
            if b.w is not None:
                deps.append(b.w)
        deps.extend(extra_deps)
        if eng in self.fence_new:
            deps.extend(self.fence_deps)
            self.fence_new.discard(eng)
        it = _It(eng, fn, deps, dma)
        it.idx = len(self.items)
        self.items.append(it)
        for b in reads:
            b.rs[eng] = it
        for b in soft_reads:
            b.rs[eng] = it
        for b in writes:
            b.w = it
            b.rs = {}
        return it

    def fence(self):
        last = {}
        for it in self.items:
            last[it.eng] = it
        self.fence_deps = list(last.values())
        self.fence_new = {"v", "s", "t", "g", "y"}

    def finalize(self):
        for it in self.items:
            if it.dma in ("dSO", "dSO1", "dSO2", "dSO3"):
                it.sig = True
            for d in it.deps:
                if d is it:
                    continue
                if d.eng == "t" and it.eng == "t" and d.dma is None:
                    continue
                d.sig = True
        cnt = {}
        for it in self.items:
            if it.sig:
                key = it.dma if it.dma else ("E", it.eng)
                cnt[key] = cnt.get(key, 0) + 1
                it.val = cnt[key] * (16 if it.dma else 1)
        known = {}
        for it in self.items:
            k = known.setdefault(it.eng, {})
            ws = {}
            for d in it.deps:
                if d is it:
                    continue
                if d.eng == "t" and it.eng == "t" and d.dma is None:
                    continue
                if not d.sig:
                    continue
                key = d.dma if d.dma else ("E", d.eng)
                if ws.get(key, 0) < d.val:
                    ws[key] = d.val
            final = []
            for key, v in ws.items():
                if k.get(key, 0) < v:
                    k[key] = v
                    final.append((key, v))
            it.waits = tuple(final)
        return cnt


def build_program(debug=False):
    nc = bass.Bass("TRN2", target_bir_lowering=False, debug=False)

    dt = {}
    def din(name, shape, dtype=F32):
        dt[name] = nc.dram_tensor(name, shape, dtype, kind="ExternalInput").ap()
        return dt[name]

    x_h = din("x", [B_CORE, T, D])
    wq_h = din("wqT", [D, D]); wk_h = din("wkT", [D, D]); wv_h = din("wvT", [D, D])
    wo_h = din("woT", [D, D])
    wr_h = din("wrT", [D, D]); wk2_h = din("wk2T", [D, D]); wv2_h = din("wv2T", [D, D])
    konst_h = din("konst", [128, 4])
    bq_h = din("bq2", [128, NCH]); bk_h = din("bk2", [128, NCH])
    bv_h = din("bv2", [128, NCH]); bo_h = din("bo2", [128, NCH])
    mur_h = din("mur2", [128, NCH]); muk_h = din("muk2", [128, NCH])
    gates_h = din("g64", [128, H])
    bv64_h = din("bv64", [128, H])
    strips_h = din("strips", [H, 128, STRIP_W])
    ident_h = din("ident", [128, 128], BF16)
    out_h = nc.dram_tensor("out", [B_CORE, T, D], F32, kind="ExternalOutput").ap()

    dbg_h = {}
    if debug:
        for nm, shp, dty in [
            ("d_xnT", [B_CORE, 128, NCH, T], F32),
            ("d_Q", [B_CORE, 128, NCH, T], F32),
            ("d_K", [B_CORE, 128, NCH, T], F32),
            ("d_V", [B_CORE, 128, NTT, D], F32),
            ("d_sb", [128, T], F32),
            ("d_e", [128, T], mybir.dt.bfloat16),
            ("d_c32", [128, 32], F32),
            ("d_attnT", [B_CORE, 128, NCH, T], F32),
            ("d_xtile", [128, 1024], F32),
            ("d_stats", [128, 16], F32),
            ("d_xntok", [128, 1024], F32),
            ("d_outT", [B_CORE, 128, NCH, T], F32),
        ]:
            dbg_h[nm] = nc.dram_tensor(nm, shp, dty, kind="ExternalOutput").ap()

    p = Plan()

    import contextlib
    with contextlib.ExitStack() as _stk:
        def _sb(name, shape, dtype):
            return _stk.enter_context(nc.sbuf_tensor(name, shape, dtype))
        AC = _sb("AC", [128, NCH, T + 4], BF16)
        Qa = _sb("Qa", [128, NCH, T], BF16)
        Ka = _sb("Ka", [128, NCH, T], BF16)
        Va = _sb("Va", [128, NTT, D], BF16)
        Ua = _sb("Ua", [128, 8192], BF16)
        Ea = _sb("Ea", [128, 2048], F32)
        ef1 = _sb("ef1", [128, 2048], F32)
        e2f = _sb("e2f", [128, 2, 2048], F32)
        et0 = _sb("et0", [128, 2048], BF16)
        et1 = _sb("et1", [128, 2048], BF16)
        eTt0 = _sb("eTt0", [128, 16, 128], BF16)
        g3 = _sb("g3", [128, 496], F32)
        xntok = et0[:, 0:1024]
        xntok2 = et0[:, 1024:2048]
        wtmp = _sb("wtmp", [128, 512], BF16)
        identt = _sb("identt", [128, 128], BF16)
        c32t = _sb("c32", [128, 32], F32)
        bq2s = _sb("bq2s", [128, NCH], F32)
        bk2s = _sb("bk2s", [128, NCH], F32)
        bo2s = _sb("bo2s", [128, NCH], F32)
        mur2s = _sb("mur2s", [128, NCH], F32)
        muk2s = _sb("muk2s", [128, NCH], F32)
        gatess = _sb("gatess", [128, H], F32)
        bv64s = _sb("bv64s", [128, H], F32)
        gbvs = _sb("gbvs", [128, H], F32)
        statst = _sb("stats", [128, 32], F32)
        konsts = _sb("konsts", [128, 4], F32)
        ps6 = _stk.enter_context(nc.psum_tensor("ps6", [128, 6, 512], F32))
        pstr = _stk.enter_context(nc.psum_tensor("pstr", [128, 2, 1024], BF16))
        SV = _stk.enter_context(nc.semaphore("SV"))
        SS = _stk.enter_context(nc.semaphore("SS"))
        STm = _stk.enter_context(nc.semaphore("ST"))
        SGm = _stk.enter_context(nc.semaphore("SG"))
        SY = _stk.enter_context(nc.semaphore("SY"))
        dE0 = _stk.enter_context(nc.semaphore("dE0"))
        dE1 = _stk.enter_context(nc.semaphore("dE1"))
        dE2 = _stk.enter_context(nc.semaphore("dE2"))
        dE3 = _stk.enter_context(nc.semaphore("dE3"))
        dG3 = _stk.enter_context(nc.semaphore("dG3"))
        dCN = _stk.enter_context(nc.semaphore("dCN"))
        dSO = _stk.enter_context(nc.semaphore("dSO"))
        dSO1 = _stk.enter_context(nc.semaphore("dSO1"))
        dE4 = _stk.enter_context(nc.semaphore("dE4"))
        dE5 = _stk.enter_context(nc.semaphore("dE5"))
        dSO2 = _stk.enter_context(nc.semaphore("dSO2"))
        dSO3 = _stk.enter_context(nc.semaphore("dSO3"))
        block = _stk.enter_context(nc.Block())
        sems = {("E", "v"): SV, ("E", "s"): SS, ("E", "t"): STm,
                ("E", "g"): SGm, ("E", "y"): SY,
                "dE0": dE0, "dE1": dE1, "dE2": dE2, "dE3": dE3,
                "dG3": dG3, "dCN": dCN, "dSO": dSO, "dSO1": dSO1,
                "dE4": dE4, "dE5": dE5, "dSO2": dSO2, "dSO3": dSO3}

        nc.const_aps.register(F32, 0.0, konsts[:, 0:1])
        nc.const_aps.register(F32, 1.0, konsts[:, 1:2])
        nc.const_aps.register(F32, 1e-5, konsts[:, 2:3])

        bAC = Buf("AC"); bQ = Buf("Q"); bK = Buf("K"); bV = Buf("V")
        bW = Buf("W"); bS0 = Buf("S0"); bS1 = Buf("S1")
        bE0 = Buf("E0"); bE1 = Buf("E1")
        bEFx = [Buf("ef0"), Buf("ef1")]
        efv = [Ea[:], ef1[:]]
        bE2Fx = [Buf("e2f0"), Buf("e2f1")]
        e2fv = [e2f[:, 0, :], e2f[:, 1, :]]
        bEx = [Buf("e0"), Buf("e1")]
        _bET = Buf("eT0")
        bETx = [_bET, _bET]
        etv = [et0[:], et1[:]]
        eTv = [eTt0, eTt0]
        bG3 = Buf("g3"); bXN = Buf("xntok"); bXN2 = Buf("xntok2")
        bWT = Buf("wtmp")
        bC32 = Buf("c32"); bE32 = Buf("e32")
        bPS = [Buf(f"ps{i}") for i in range(6)]
        bPSS = Buf("pss")
        bTR0 = Buf("tr0"); bTR1 = Buf("tr1")
        bCN = Buf("consts")
        bST = [Buf(f"st{i}") for i in range(32)]
        bGBV = Buf("gbv")

        W8 = Ua[:].rearrange("p (c n) -> p c n", n=1024)
        strip0 = Ua[:, 0:STRIP_W]
        strip1 = Ua[:, 4096:4096 + STRIP_W]
        psS_flat = ps6[:, 0:4, :].rearrange("p a b -> p (a b)")

        def ld(dst, src):
            return lambda e, d=dst, s=src: e.dma_start(out=d, in_=s)

        _prev = None
        for dst, src in [(konsts[:], konst_h[:, :]),
                         (bq2s[:], bq_h[:, :]), (bk2s[:], bk_h[:, :]),
                         (bo2s[:], bo_h[:, :]),
                         (mur2s[:], mur_h[:, :]), (muk2s[:], muk_h[:, :]),
                         (gatess[:], gates_h[:, :]), (bv64s[:], bv64_h[:, :]),
                         (identt[:], ident_h[:, :])]:
            _prev = p.op("y", ld(dst, src), writes=[bCN], dma="dCN",
                         extra_deps=[_prev] if _prev else ())
        del _prev

        p.op("v", lambda e: e.memset(AC[:, :, 0:4], 0.0), writes=[bAC])

        p.op("v", lambda e: e.tensor_tensor(
            gbvs[:], gatess[:], bv64s[:], ALU.mult),
            reads=[bCN], writes=[bGBV])

        wslotv = [Ea[:, 0:1024], Ea[:, 1024:2048],
                  e2fv[0][:, 0:1024], e2fv[1][:, 0:1024]]
        wslotb = [bE0, bE1, bE2Fx[0], bE2Fx[1]]

        def load_weight(wh):
            for blk in range(8):
                sl = blk % 4
                ebuf = wslotb[sl]
                dst = wslotv[sl]
                p.op(("y", "g", "y", "g")[sl],
                     ld(dst, wh[128 * blk:128 * blk + 128, :]),
                     writes=[ebuf], dma=("dE0", "dE4", "dE3", "dE5")[sl])
                p.op("s", lambda e, d=blk, src=dst: e.copy(W8[:, d, :], src),
                     reads=[ebuf], writes=[bW, bS0, bS1])

        def ln_pass(b, dst_buf, dst_view, xnbufs, xnts, dbg=None, p_=None,
                    dbg_bufs=None):
            st = statst
            for tt in range(NTT):
                eslot = tt % 2
                ebuf = xnbufs[2][eslot]
                xsl = Ea[:, 0:1024] if eslot == 0 else Ea[:, 1024:2048]
                p.op("y" if eslot == 0 else "g",
                     ld(xsl, x_h[b, 128 * tt:128 * tt + 128, :]),
                     writes=[ebuf], dma="dE0" if eslot == 0 else "dE2")
                so = 9 * (tt % 2)
                bsum, bsq, bnm, bvar, brstd, bnmr = (bST[so + i] for i in range(6))
                xnb = xnbufs[0] if tt % 2 == 0 else xnbufs[1]
                xnt = xnts[0] if tt % 2 == 0 else xnts[1]
                p.op("v", lambda e, x=xsl: e.reduce_sum(st[:, so + 0:so + 1], x, axis=AX.X),
                     reads=[ebuf], writes=[bsum])
                p.op("s", lambda e, x=xsl, o=xnt: e.activation(
                    o[:], x, AF.Square, accum_out=st[:, so + 1:so + 2]),
                    reads=[ebuf], writes=[bsq, xnb])
                p.op("v", lambda e: e.tensor_scalar(
                    st[:, so + 2:so + 3], st[:, so + 0:so + 1], -1.0 / D, None, ALU.mult),
                    reads=[bsum], writes=[bnm])
                p.op("v", lambda e: e.tensor_tensor(
                    st[:, so + 3:so + 4], st[:, so + 2:so + 3], st[:, so + 2:so + 3], ALU.mult),
                    reads=[bnm], writes=[bvar])
                p.op("v", lambda e: e.tensor_scalar(
                    st[:, so + 4:so + 5], st[:, so + 1:so + 2], 1.0 / D, None, ALU.mult,
                    accum_out=None), reads=[bsq], writes=[bvar])
                p.op("v", lambda e: e.tensor_tensor(
                    st[:, so + 5:so + 6], st[:, so + 4:so + 5], st[:, so + 3:so + 4], ALU.subtract),
                    reads=[bvar], writes=[bvar])
                p.op("s", lambda e: e.activation(
                    st[:, so + 6:so + 7], st[:, so + 5:so + 6], AF.Sqrt, bias=1e-5),
                    reads=[bvar], writes=[brstd])
                p.op("v", lambda e: e.reciprocal(st[:, so + 7:so + 8], st[:, so + 6:so + 7]),
                     reads=[brstd], writes=[brstd])
                p.op("v", lambda e: e.tensor_tensor(
                    st[:, so + 8:so + 9], st[:, so + 2:so + 3], st[:, so + 7:so + 8], ALU.mult),
                    reads=[bnm, brstd], writes=[bnmr])
                p.op("s", lambda e, x=xsl, o=xnt: e.activation(
                    o[:], x, AF.Identity, bias=st[:, so + 8:so + 9], scale=st[:, so + 7:so + 8]),
                    reads=[ebuf, bnmr, brstd], writes=[xnb])
                trb = bTR0 if tt % 2 == 0 else bTR1
                bank = 0 if tt % 2 == 0 else 1
                for ci in range(8):
                    p.op("t", lambda e, ci=ci, bank=bank, xnt=xnt: e.matmul(
                        pstr[:, bank, 128 * ci:128 * ci + 128],
                        xnt[:, 128 * ci:128 * ci + 128], identt[:],
                        is_transpose=True, start=(ci == 0), stop=(ci == 7),
                        skip_group_check=True),
                        reads=[xnb, bCN], writes=[trb])
                p.op("v", lambda e, tt=tt, bank=bank, dv=dst_view: e.tensor_copy(
                    dv[:, :, 4 + 128 * tt:4 + 128 * tt + 128],
                    pstr[:, bank, :].rearrange("p (c n) -> p c n", n=128)),
                    reads=[trb], writes=[dst_buf])
                if dbg is not None and tt == 0:
                    p.op("g", lambda e, x=xsl: e.dma_start(
                        out=dbg["d_xtile"][:, :], in_=x), reads=[dbg_bufs[0]],
                        dma="dSO")
                    p.op("g", lambda e: e.dma_start(
                        out=dbg["d_stats"][:, :], in_=st[:]),
                        reads=list(dbg_bufs[1][:9]), dma="dSO")
                    p.op("g", lambda e, o=xnt: e.dma_start(
                        out=dbg["d_xntok"][:, :], in_=o[:]),
                        reads=[dbg_bufs[2]], dma="dSO")

        for b in range(B_CORE):
            load_weight(wq_h)
            ln_pass(b, bAC, AC[:], (bXN, bXN2, (bE0, bE1)), (xntok[:], xntok2[:]),
                    dbg=(dbg_h if (debug and b == 0) else None), p_=p,
                    dbg_bufs=(bE0, bST, bXN))
            if debug:
                p.op("g", lambda e, b=b: e.dma_start(
                    out=dbg_h["d_xnT"][b], in_=AC[:, :, 4:4 + T]),
                    reads=[bAC], dma="dSO")

            xnT = AC[:]
            for wi_, (wh, dstv, dstb, bias_s, scale) in enumerate([
                (wq_h, Qa[:], bQ, bq2s, 0.125),
                (wk_h, Ka[:], bK, bk2s, None),
            ]):
                if wi_ > 0:
                    load_weight(wh)
                for oc in range(8):
                    for th in range(4):
                        pb = bPS[4 + (oc * 4 + th) % 2]
                        bank = 4 + (oc * 4 + th) % 2
                        for kc in range(8):
                            p.op("t", lambda e, kc=kc, oc=oc, th=th, bank=bank:
                                 e.matmul(ps6[:, bank, :],
                                          W8[:, kc, 128 * oc:128 * oc + 128],
                                          xnT[:, kc, 4 + 512 * th:4 + 512 * th + 512],
                                          start=(kc == 0), stop=(kc == 7)),
                                 reads=[bW, bAC], writes=[pb])
                        if scale is None:
                            p.op("v", lambda e, oc=oc, th=th, bank=bank, dv=dstv,
                                 bs=bias_s: e.tensor_scalar(
                                     dv[:, oc, 512 * th:512 * th + 512],
                                     ps6[:, bank, :], bs[:, oc:oc + 1], None,
                                     ALU.add),
                                 reads=[pb, bCN], writes=[dstb])
                        else:
                            p.op("v", lambda e, oc=oc, th=th, bank=bank, dv=dstv,
                                 bs=bias_s, sc=scale: e.tensor_scalar(
                                     dv[:, oc, 512 * th:512 * th + 512],
                                     ps6[:, bank, :], bs[:, oc:oc + 1], sc,
                                     ALU.add, ALU.mult),
                                 reads=[pb, bCN], writes=[dstb])
            load_weight(wv_h)
            for tt in range(NTT):
                for dh in range(2):
                    bank = 4 + (tt * 2 + dh) % 2
                    pb = bPS[bank]
                    for kc in range(8):
                        p.op("t", lambda e, kc=kc, tt=tt, dh=dh, bank=bank:
                             e.matmul(ps6[:, bank, :],
                                      xnT[:, kc, 4 + 128 * tt:4 + 128 * tt + 128],
                                      W8[:, kc, 512 * dh:512 * dh + 512],
                                      start=(kc == 0), stop=(kc == 7)),
                             reads=[bW, bAC], writes=[pb])
                    p.op("s", lambda e, tt=tt, dh=dh, bank=bank: e.copy(
                        Va[:, tt, 512 * dh:512 * dh + 512], ps6[:, bank, :]),
                        reads=[pb], writes=[bV])
            if debug:
                p.op("g", lambda e, b=b: e.dma_start(
                    out=dbg_h["d_Q"][b], in_=Qa[:]), reads=[bQ], dma="dSO")
                p.op("g", lambda e, b=b: e.dma_start(
                    out=dbg_h["d_K"][b], in_=Ka[:]), reads=[bK], dma="dSO")
                p.op("g", lambda e, b=b: e.dma_start(
                    out=dbg_h["d_V"][b], in_=Va[:]), reads=[bV], dma="dSO")

            attnT = AC[:]
            st = statst

            def load_strip(h):
                slot = h % 2
                sbuf = bS0 if slot == 0 else bS1
                sview = strip0 if slot == 0 else strip1
                for q4 in range(8):
                    w0 = 496 * q4
                    w1 = min(STRIP_W, w0 + 496)
                    p.op("y", lambda e, h=h, w0=w0, w1=w1: e.dma_start(
                        out=g3[:, 0:w1 - w0], in_=strips_h[h, :, w0:w1]),
                        writes=[bG3], dma="dG3")
                    p.op("s", lambda e, w0=w0, w1=w1, sv=sview: e.copy(
                        sv[:, w0:w1], g3[:, 0:w1 - w0]),
                        reads=[bG3], writes=[sbuf, bW])

            tiles = [(h, qt) for h in range(H) for qt in range(NTT)]

            def stage_a(i):
                h, qt = tiles[i]
                hc, hp = h // 2, h % 2
                sbuf = bS0 if h % 2 == 0 else bS1
                sview = strip0 if h % 2 == 0 else strip1
                pi = i % 2
                off = 1920 - 128 * qt
                for kc in range(4):
                    p.op("t", lambda e, kc=kc, qt=qt, hc=hc, hp=hp: e.matmul(
                        ps6[:, kc, :],
                        Qa[64 * hp:64 * hp + 64, hc, 128 * qt:128 * qt + 128],
                        Ka[64 * hp:64 * hp + 64, hc, 512 * kc:512 * kc + 512],
                        start=True, stop=False, skip_group_check=True),
                        reads=[bQ, bK], writes=[bPSS])
                for kc in range(4):
                    p.op("t", lambda e, kc=kc, off=off, sv=sview: e.matmul(
                        ps6[:, kc, :], identt[:],
                        sv[:, off + 512 * kc:off + 512 * kc + 512],
                        start=False, stop=True, skip_group_check=True),
                        reads=[sbuf, bCN], writes=[bPSS])
                wl = [bEFx[pi]] + ([bE0, bE1] if pi == 0 else [])
                q4 = i % 4
                p.op("s", lambda e, efp=efv[pi], q4=q4: e.activation(
                    efp, psS_flat, AF.Exp,
                    accum_out=st[:, 24 + q4:25 + q4]),
                    reads=[bPSS], writes=wl + [bST[24 + q4]])

            load_strip(0)
            stage_a(0)
            n_t = len(tiles)

            def rounds_a(i):
                pi = i % 2
                efp, e2fp = efv[pi], e2fv[pi]
                bE2Fp = bE2Fx[pi]
                rEF = [bEFx[pi]] + ([bE0, bE1] if pi == 0 else [])
                p.op("v", lambda e, efp=efp: e.max(out=c32t[:, 0:8], in_=efp),
                     reads=rEF, writes=[bC32])
                p.op("v", lambda e, efp=efp, e2fp=e2fp: e.match_replace(
                    out=e2fp, in_to_replace=c32t[:, 0:8],
                    in_values=efp, imm_value=0.0),
                    reads=[bC32] + rEF, writes=[bE2Fp])
                for rnd in range(1, 3):
                    p.op("v", lambda e, r=rnd, e2fp=e2fp: e.max(
                        out=c32t[:, 8 * r:8 * r + 8], in_=e2fp),
                        reads=[bE2Fp], writes=[bC32])
                    p.op("v", lambda e, r=rnd, e2fp=e2fp: e.match_replace(
                        out=e2fp, in_to_replace=c32t[:, 8 * r:8 * r + 8],
                        in_values=e2fp, imm_value=0.0),
                        reads=[bC32], writes=[bE2Fp])

            def rounds_b(i):
                pi = i % 2
                efp, e2fp = efv[pi], e2fv[pi]
                bE2Fp = bE2Fx[pi]
                rEF = [bEFx[pi]] + ([bE0, bE1] if pi == 0 else [])
                rEp = [bEx[pi]] + ([bXN, bXN2] if pi == 0 else [])
                ep = etv[pi]
                p.op("v", lambda e, e2fp=e2fp: e.max(
                    out=c32t[:, 24:32], in_=e2fp),
                    reads=[bE2Fp], writes=[bC32])
                p.op("v", lambda e, e2fp=e2fp: e.match_replace(
                    out=e2fp, in_to_replace=c32t[:, 24:32],
                    in_values=e2fp, imm_value=0.0),
                    reads=[bC32], writes=[bE2Fp])
                q4 = i % 4
                p.op("s", lambda e, ep=ep, e2fp=e2fp, q4=q4: e.activation(
                    ep, e2fp, AF.Identity,
                    accum_out=st[:, 28 + q4:29 + q4]),
                    reads=[bE2Fp], writes=rEp + [bST[28 + q4]])
                p.op("g", lambda e, ep=ep, efp=efp, e2fp=e2fp: e.tensor_tensor(
                    ep, efp, e2fp, ALU.subtract),
                    reads=[bE2Fp] + rEF, writes=rEp)

            def tail_zr(i):
                pi = i % 2
                q4 = i % 4
                p.op("v", lambda e, pi=pi, q4=q4: e.tensor_tensor(
                    st[:, 18 + pi:19 + pi], st[:, 24 + q4:25 + q4],
                    st[:, 28 + q4:29 + q4], ALU.subtract),
                    reads=[bST[24 + q4], bST[28 + q4]], writes=[bST[18 + pi]])
                p.op("v", lambda e, pi=pi: e.reciprocal(
                    st[:, 20 + pi:21 + pi], st[:, 18 + pi:19 + pi]),
                    reads=[bST[18 + pi]], writes=[bST[20 + pi]])

            def tail_scale(i):
                pi = i % 2
                rEp = [bEx[pi]] + ([bXN, bXN2] if pi == 0 else [])
                ep = etv[pi]
                p.op("v", lambda e, ep=ep, pi=pi: e.tensor_scalar(
                    ep, ep, st[:, 20 + pi:21 + pi], None, ALU.mult),
                    reads=[bST[20 + pi]] + rEp, writes=rEp)

            def tail_rest(i):
                h, qt = tiles[i]
                hc, hp = h // 2, h % 2
                pi = i % 2
                bEp, bETp = bEx[pi], bETx[pi]
                ep, eTp = etv[pi], eTv[pi]
                rEp = [bEp] + ([bXN, bXN2] if pi == 0 else [])
                if debug and h == 0 and qt == 0 and b == 0:
                    p.op("g", lambda e, ep=ep: e.dma_start(
                        out=dbg_h["d_e"][:, :], in_=ep),
                        reads=[bEp], dma="dSO")
                for g in range(2):
                    trb = bTR0 if g == 0 else bTR1
                    for j in range(8):
                        ci = 8 * g + j
                        p.op("t", lambda e, ci=ci, g=g, j=j, ep=ep: e.matmul(
                            pstr[:, g, 128 * j:128 * j + 128],
                            ep[:, 128 * ci:128 * ci + 128], identt[:],
                            is_transpose=True, start=(j == 0), stop=(j == 7),
                            skip_group_check=True),
                            reads=rEp + [bCN], writes=[trb])
                    p.op("s", lambda e, g=g, eTp=eTp: e.copy(
                        eTp[:, 8 * g:8 * g + 8, :],
                        pstr[:, g, :].rearrange("p (c n) -> p c n", n=128)),
                        reads=[trb], writes=[bETp])
                bank = 4 + pi
                pb = bPS[bank]
                for c in range(16):
                    p.op("t", lambda e, c=c, h=h, bank=bank, eTp=eTp: e.matmul(
                        ps6[0:64, bank, 0:128],
                        Va[:, c, 64 * h:64 * h + 64],
                        eTp[:, c, :],
                        start=(c == 0), stop=(c == 15)),
                        reads=[bV, bETp], writes=[pb])
                p.op("s", lambda e, h=h, hc=hc, hp=hp, qt=qt, bank=bank:
                     e.activation(
                         attnT[64 * hp:64 * hp + 64, hc,
                               4 + 128 * qt:4 + 128 * qt + 128],
                         ps6[0:64, bank, 0:128], AF.Identity,
                         scale=gatess[64 * hp:64 * hp + 64, h:h + 1],
                         bias=gbvs[64 * hp:64 * hp + 64, h:h + 1]),
                     reads=[pb, bCN, bGBV], writes=[bAC])

            for i, (h, qt) in enumerate(tiles):
                if qt == 0 and h + 1 < H:
                    load_strip(h + 1)
                rounds_a(i)
                if i >= 1:
                    tail_zr(i - 1)
                if i + 1 < n_t:
                    stage_a(i + 1)
                rounds_b(i)
                if debug and i == 0 and b == 0:
                    p.op("g", lambda e: e.dma_start(
                        out=dbg_h["d_c32"][:, :], in_=c32t[:]),
                        reads=[bC32], dma="dSO")
                if i >= 1:
                    tail_scale(i - 1)
                if i >= 1:
                    tail_rest(i - 1)
            tail_zr(n_t - 1)
            tail_scale(n_t - 1)
            tail_rest(n_t - 1)
            if debug:
                p.op("g", lambda e, b=b: e.dma_start(
                    out=dbg_h["d_attnT"][b], in_=attnT[:, :, 4:4 + T]),
                    reads=[bAC], dma="dSO")

            outT = Va[:].rearrange("p a b -> p (a b)").rearrange(
                "p (c n) -> p c n", n=T)
            bOT = bV
            load_weight(wo_h)
            for oc in range(8):
                for th in range(4):
                    bank = 4 + (oc * 4 + th) % 2
                    pb = bPS[bank]
                    for kc in range(8):
                        p.op("t", lambda e, kc=kc, oc=oc, th=th, bank=bank:
                             e.matmul(ps6[:, bank, :],
                                      W8[:, kc, 128 * oc:128 * oc + 128],
                                      attnT[:, kc, 4 + 512 * th:4 + 512 * th + 512],
                                      start=(kc == 0), stop=(kc == 7)),
                             reads=[bW, bAC], writes=[pb])
                    p.op("v", lambda e, oc=oc, th=th, bank=bank: e.tensor_scalar(
                        outT[:, oc, 512 * th:512 * th + 512],
                        ps6[:, bank, :], bo2s[:, oc:oc + 1], None, ALU.add),
                        reads=[pb, bCN], writes=[bOT])

            load_weight(wr_h)
            ln_pass(b, bAC, AC[:], (bXN, bXN2, (bE0, bE1)), (xntok[:], xntok2[:]))
            xnT2 = AC[:]
            xr = Qa[:]; bXR = bQ
            xk = Ka[:]; bXK = bK
            for ts_ in range(4):
                t0 = 4 + 512 * ts_
                for ci in range(8):
                    dsl = et0[:, 512 * (ci % 4):512 * (ci % 4) + 512]
                    p.op("v", lambda e, ci=ci, t0=t0, dsl=dsl: e.tensor_tensor(
                        dsl, xnT2[:, ci, t0:t0 + 512],
                        xnT2[:, ci, t0 - 1:t0 + 511], ALU.subtract),
                        reads=[bAC], writes=[bEx[0], bXN, bXN2])
                    p.op("v", lambda e, ci=ci, t0=t0, ts_=ts_, dsl=dsl:
                         e.scalar_tensor_tensor(
                             xr[:, ci, t0 - 4:t0 - 4 + 512], dsl,
                             mur2s[:, ci:ci + 1],
                             xnT2[:, ci, t0 - 1:t0 + 511], ALU.mult, ALU.add),
                         reads=[bEx[0], bAC, bCN], writes=[bXR])
                    p.op("v", lambda e, ci=ci, t0=t0, ts_=ts_, dsl=dsl:
                         e.scalar_tensor_tensor(
                             xk[:, ci, t0 - 4:t0 - 4 + 512], dsl,
                             muk2s[:, ci:ci + 1],
                             xnT2[:, ci, t0 - 1:t0 + 511], ALU.mult, ALU.add),
                         reads=[bEx[0], bAC, bCN], writes=[bXK])
            rT = AC[:]
            for oc in range(8):
                for th in range(4):
                    bank = 4 + (oc * 4 + th) % 2
                    pb = bPS[bank]
                    for kc in range(8):
                        p.op("t", lambda e, kc=kc, oc=oc, th=th, bank=bank:
                             e.matmul(ps6[:, bank, :],
                                      W8[:, kc, 128 * oc:128 * oc + 128],
                                      xr[:, kc, 512 * th:512 * th + 512],
                                      start=(kc == 0), stop=(kc == 7)),
                             reads=[bW, bXR], writes=[pb])
                    p.op("s", lambda e, oc=oc, th=th, bank=bank: e.activation(
                        rT[:, oc, 4 + 512 * th:4 + 512 * th + 512],
                        ps6[:, bank, :], AF.Sigmoid),
                        reads=[pb], writes=[bAC])
            load_weight(wk2_h)
            kaT = Qa[:]
            for oc in range(8):
                for th in range(4):
                    bank = 4 + (oc * 4 + th) % 2
                    pb = bPS[bank]
                    for kc in range(8):
                        p.op("t", lambda e, kc=kc, oc=oc, th=th, bank=bank:
                             e.matmul(ps6[:, bank, :],
                                      W8[:, kc, 128 * oc:128 * oc + 128],
                                      xk[:, kc, 512 * th:512 * th + 512],
                                      start=(kc == 0), stop=(kc == 7)),
                             reads=[bW, bXK], writes=[pb])
                    p.op("v", lambda e, oc=oc, th=th, bank=bank: e.tensor_scalar(
                        wtmp[:], ps6[:, bank, :], 0.0, None, ALU.max),
                        reads=[pb], writes=[bWT])
                    p.op("v", lambda e, oc=oc, th=th: e.tensor_tensor(
                        kaT[:, oc, 512 * th:512 * th + 512], wtmp[:], wtmp[:],
                        ALU.mult),
                        reads=[bWT], writes=[bXR])
            load_weight(wv2_h)
            for oc in range(8):
                for th in range(4):
                    bank = 4 + (oc * 4 + th) % 2
                    pb = bPS[bank]
                    for kc in range(8):
                        p.op("t", lambda e, kc=kc, oc=oc, th=th, bank=bank:
                             e.matmul(ps6[:, bank, :],
                                      W8[:, kc, 128 * oc:128 * oc + 128],
                                      kaT[:, kc, 512 * th:512 * th + 512],
                                      start=(kc == 0), stop=(kc == 7)),
                             reads=[bW, bXR], writes=[pb])
                    p.op("v", lambda e, oc=oc, th=th, bank=bank: e.tensor_tensor(
                        wtmp[:], ps6[:, bank, :],
                        rT[:, oc, 4 + 512 * th:4 + 512 * th + 512], ALU.mult),
                        reads=[pb, bAC], writes=[bWT])
                    p.op("v", lambda e, oc=oc, th=th: e.tensor_tensor(
                        outT[:, oc, 512 * th:512 * th + 512],
                        outT[:, oc, 512 * th:512 * th + 512], wtmp[:], ALU.add),
                        reads=[bWT], writes=[bOT])
            if debug:
                p.op("g", lambda e, b=b: e.dma_start(
                    out=dbg_h["d_outT"][b], in_=outT[:]), reads=[bOT], dma="dSO")

            x6v = [Ea[:, 0:1024], Ea[:, 1024:2048],
                   e2fv[0][:, 0:1024], e2fv[1][:, 0:1024]]
            x6b = [bE0, bE1, bE2Fx[0], bE2Fx[1]]
            for tt in range(NTT):
                eslot = tt % 4
                ebuf = x6b[eslot]
                xsl = x6v[eslot]
                p.op(("y", "g", "y", "g")[eslot],
                     ld(xsl, x_h[b, 128 * tt:128 * tt + 128, :]),
                     writes=[ebuf], dma=("dE0", "dE4", "dE3", "dE5")[eslot])
                bank = tt % 2
                trb = bTR0 if bank == 0 else bTR1
                for oc in range(8):
                    p.op("t", lambda e, oc=oc, tt=tt, bank=bank: e.matmul(
                        pstr[:, bank, 128 * oc:128 * oc + 128],
                        outT[:, oc, 128 * tt:128 * tt + 128], identt[:],
                        is_transpose=True, start=(oc == 0), stop=(oc == 7),
                        skip_group_check=True),
                        reads=[bOT, bCN], writes=[trb])
                p.op("v", lambda e, xsl=xsl, bank=bank: e.tensor_tensor(
                    xsl, xsl, pstr[:, bank, :], ALU.add),
                    reads=[ebuf, trb], writes=[ebuf])
                ssem = ("dSO", "dSO1", "dSO2", "dSO3")[eslot]
                it = p.op(("s", "s", "y", "s")[eslot],
                          lambda e, b=b, tt=tt, xsl=xsl: e.dma_start(
                    out=out_h[b, 128 * tt:128 * tt + 128, :], in_=xsl),
                    reads=[ebuf], dma=ssem)
                p.store_items.append(it)

        cnt = p.finalize()
        n_store = cnt.get("dSO", 0)
        n_store1 = cnt.get("dSO1", 0)
        n_g3 = cnt.get("dSO2", 0)
        n_cn = cnt.get("dSO3", 0)

        def emit(block_eng, eng_key, eng_obj_getter=None):
            def body(eng):
                for it in p.items:
                    if it.eng != eng_key:
                        continue
                    for key, v in it.waits:
                        eng.wait_ge(sems[key], v)
                    ins = it.fn(eng)
                    if it.sig:
                        key = it.dma if it.dma else ("E", it.eng)
                        ins.then_inc(sems[key], 16 if it.dma else 1)
                if eng_key == "g":
                    eng.wait_ge(sems["dSO"], 16 * n_store)
                    eng.wait_ge(sems["dSO1"], 16 * n_store1)
                    eng.wait_ge(sems["dSO2"], 16 * n_g3)
                    eng.wait_ge(sems["dSO3"], 16 * n_cn)
            return body

        block.sync(emit(block, "y"))
        block.vector(emit(block, "v"))
        block.scalar(emit(block, "s"))
        block.tensor(emit(block, "t"))
        block.gpsimd(emit(block, "g"))

    return nc


def _stage_inputs(L0_attn_x, Wq, bq, Wk, bk, Wv, bv, Wo, bo, head_gates,
                  rel_bias, W_r, W_k2, W_v2, mu_r, mu_k):
    f32 = np.float32
    def feat(v):
        return np.ascontiguousarray(v.reshape(NCH, 128).T.astype(f32))
    common = {
        "wqT": np.ascontiguousarray(Wq.T.astype(f32)),
        "wkT": np.ascontiguousarray(Wk.T.astype(f32)),
        "wvT": np.ascontiguousarray(Wv.T.astype(f32)),
        "woT": np.ascontiguousarray(Wo.T.astype(f32)),
        "wrT": np.ascontiguousarray(W_r.T.astype(f32)),
        "wk2T": np.ascontiguousarray(W_k2.T.astype(f32)),
        "wv2T": np.ascontiguousarray(W_v2.T.astype(f32)),
        "bq2": feat(bq), "bk2": feat(bk), "bv2": feat(bv), "bo2": feat(bo),
        "mur2": feat(mu_r), "muk2": feat(mu_k),
        "g64": np.ascontiguousarray(
            np.repeat(head_gates.astype(f32)[None, :], 128, axis=0)),
        "bv64": np.ascontiguousarray(
            np.tile(bv.astype(f32).reshape(H, 64).T, (2, 1))),
        "ident": np.eye(128, dtype=ml_dtypes.bfloat16),
        "konst": np.tile(np.array([0.0, 1.0, 1e-5, 0.0], dtype=f32), (128, 1)),
    }
    cc = (np.arange(STRIP_W)[None, :] - np.arange(128)[:, None] - 1409)
    idx = np.clip(cc, 0, 2 * 512 - 2)
    common["strips"] = np.ascontiguousarray(
        rel_bias.astype(f32)[idx, :].transpose(2, 0, 1))
    x = np.asarray(L0_attn_x, dtype=f32)
    in_maps = []
    for c in range(NCORES):
        m = dict(common)
        m["x"] = np.ascontiguousarray(x[B_CORE * c:B_CORE * (c + 1)])
        in_maps.append(m)
    return in_maps


def kernel(L0_attn_x, L0_attn_Wq, L0_attn_bq, L0_attn_Wk, L0_attn_bk,
           L0_attn_Wv, L0_attn_bv, L0_attn_Wo, L0_attn_bo,
           L0_attn_head_gates, L0_attn_rel_bias,
           L0_ffn_W_r, L0_ffn_W_k, L0_ffn_W_v, L0_ffn_mu_r, L0_ffn_mu_k):
    in_maps = _stage_inputs(
        np.asarray(L0_attn_x), np.asarray(L0_attn_Wq), np.asarray(L0_attn_bq),
        np.asarray(L0_attn_Wk), np.asarray(L0_attn_bk),
        np.asarray(L0_attn_Wv), np.asarray(L0_attn_bv),
        np.asarray(L0_attn_Wo), np.asarray(L0_attn_bo),
        np.asarray(L0_attn_head_gates), np.asarray(L0_attn_rel_bias),
        np.asarray(L0_ffn_W_r), np.asarray(L0_ffn_W_k), np.asarray(L0_ffn_W_v),
        np.asarray(L0_ffn_mu_r), np.asarray(L0_ffn_mu_k))
    nc = build_program(debug=False)
    res = run_bass_kernel_spmd(nc, in_maps, core_ids=list(range(NCORES)))
    out = np.concatenate([res.results[c]["out"] for c in range(NCORES)], axis=0)
    return out.astype(np.float32)
